# version 11
# baseline (speedup 1.0000x reference)
import math
import os
import numpy as np
import ml_dtypes
from contextlib import ExitStack

import concourse.bass as bass
import concourse.mybir as mybir
from concourse.bass_utils import run_bass_kernel_spmd

F32 = mybir.dt.float32
BF16 = mybir.dt.bfloat16
FP16 = mybir.dt.float16
U16 = mybir.dt.uint16
AF = mybir.ActivationFunctionType
OP = mybir.AluOpType
AX = mybir.AxisListType

T = 2048
D = 2048
I = 5632
GU = 2 * I
NKC = 16
NTT = 16
LN_EPS = 1e-5
RMS_EPS = 1e-6
ATT_EPS = 1e-15
FTC = 512
NFT = T // FTC
QSP = 256
NQS = D // QSP


class Sem:
    def __init__(self, nc, ctx, name):
        self.h = ctx.enter_context(nc.semaphore(name))
        self.v = 0

    def inc(self, inst, n=1):
        inst.then_inc(self.h, n)
        self.v += n
        return self.v


def build():
    STAGE = int(os.environ.get("KSTAGE", "99"))
    DQOFF = os.environ.get("KDQOFF", "") == "1"
    QLIM = int(os.environ.get("KQLIM", "8"))
    KVLIM = int(os.environ.get("KKVLIM", "8"))
    TTLIM = int(os.environ.get("KTTLIM", str(NTT)))
    QMODE = os.environ.get("KQMODE", "full")
    YDUMP = os.environ.get("KYDUMP", "") == "1"
    WDBG = os.environ.get("KWDBG", "") == "1"
    nc = bass.Bass("TRN2", target_bir_lowering=False, debug=False,
                   detect_race_conditions=(os.environ.get("KRACE", "1") == "1"))
    dt = nc.dram_tensor
    x_h = dt("x", [T, D], F32, kind="ExternalInput").ap()
    wqkv_h = dt("wqkv", [3 * D, D], F32, kind="ExternalInput").ap()
    wproj_h = dt("wproj", [D, D], F32, kind="ExternalInput").ap()
    bproj_h = dt("bproj", [1, D], F32, kind="ExternalInput").ap()
    gup16_h = dt("gup16", [GU, 512], FP16, kind="ExternalInput").ap()
    dwn16_h = dt("dwn16", [D, 1408], FP16, kind="ExternalInput").ap()
    gsrep_h = dt("gsrep", [512, GU], FP16, kind="ExternalInput").ap()
    dsrep_h = dt("dsrep", [1408, D], FP16, kind="ExternalInput").ap()
    eye_h = dt("eye", [128, 128], BF16, kind="ExternalInput").ap()
    maskbd_h = dt("maskbd", [128, 128], BF16, kind="ExternalInput").ap()
    mask4_h = dt("mask4", [128, 4], BF16, kind="ExternalInput").ap()
    mask4t_h = dt("mask4t", [4, 128], BF16, kind="ExternalInput").ap()
    onesc_h = dt("onesc", [128, 1], BF16, kind="ExternalInput").ap()
    onesr_h = dt("onesr", [1, 128], BF16, kind="ExternalInput").ap()
    xbf = dt("xbf", [T, D], BF16, kind="Internal").ap()
    wqkvbf = dt("wqkvbf", [3 * D, D], BF16, kind="Internal").ap()
    wprojbf = dt("wprojbf", [D, D], BF16, kind="Internal").ap()
    x1f = dt("x1f", [T, D], F32, kind="ExternalOutput" if WDBG else "Internal").ap()
    x1bf = dt("x1bf", [T, D], BF16,
              kind="ExternalOutput" if WDBG else "Internal").ap()
    wkind = "ExternalOutput" if WDBG else "Internal"
    wgt = dt("wgt", [16, 128, I], BF16, kind=wkind).ap()
    wup = dt("wup", [16, 128, I], BF16, kind=wkind).ap()
    wdt = dt("wdt", [44, 128, D], BF16, kind=wkind).ap()
    cin = dt("cin", [128, 2064], F32, kind="Internal").ap()
    cfull = dt("cfull", [128, 2064], F32, kind="Internal").ap()
    out_h = dt("out", [T, D], F32, kind="ExternalOutput").ap()
    dbg_h = dt("dbg", [128, 8192], F32, kind="ExternalOutput").ap()

    ctx = ExitStack()
    with ctx:
        def ten(st, name, shape, dtp):
            return st.enter_context(nc.sbuf_tensor(name, shape, dtp))
        eye = ten(ctx, "eye_t", [128, 128], BF16)
        maskbd = ten(ctx, "maskbd_t", [128, 128], BF16)
        mask4 = ten(ctx, "mask4_t", [128, 4], BF16)
        mask4t = ten(ctx, "mask4t_t", [4, 128], BF16)
        onesc = ten(ctx, "onesc_t", [128, 1], BF16)
        onesr = ten(ctx, "onesr_t", [1, 128], BF16)
        cconst = ten(ctx, "cconst", [128, 4], U16)
        ecol = ten(ctx, "ecol", [128, 4], F32)
        stats = ten(ctx, "stats", [128, 16], F32)
        dq_u = ten(ctx, "dq_u", [128, 512], FP16)
        dq_s = ten(ctx, "dq_s", [128, 512], FP16)
        dq_m = ten(ctx, "dq_m", [128, 512], U16)
        dq_t2 = ten(ctx, "dq_t2", [128, 512], U16)
        dq_sb = ten(ctx, "dq_sb", [128, 512], U16)
        dq_o = [ten(ctx, f"dq_o{i}", [128, 2048], BF16) for i in range(2)]
        vkf = ten(ctx, "vkf", [128, 2064], F32)
        pb = [ctx.enter_context(nc.psum_tensor(f"pb{i}", [128, 512], F32)) for i in range(6)]
        pT = ctx.enter_context(nc.psum_tensor("pT", [128, 128], BF16))
        pks = ctx.enter_context(nc.psum_tensor("pks", [128, 16], F32))

        s_ld = Sem(nc, ctx, "s_ld")
        s_cast = Sem(nc, ctx, "s_cast")
        s_dqst = Sem(nc, ctx, "s_dqst")
        s_dqv = Sem(nc, ctx, "s_dqv")
        s_mm = Sem(nc, ctx, "s_mm")
        s_ev = Sem(nc, ctx, "s_ev")
        s_vv = Sem(nc, ctx, "s_vv")
        s_st = Sem(nc, ctx, "s_st")
        s_cc = Sem(nc, ctx, "s_cc")
        s_out = Sem(nc, ctx, "s_out")

        s1a = ExitStack()
        q_fm = ten(s1a, "q_fm", [128, NKC * T], BF16)
        bias_bc = ten(s1a, "bias_bc", [128, D], F32)
        bprow = vkf[32:33, 0:D]
        bprow_b = vkf[0:1, :].bitcast(BF16)[:, 0:D]
        s1b = ExitStack()
        x_fm = ten(s1b, "x_fm", [128, NKC * T], BF16)
        wsl = [ten(s1b, f"wsl{i}", [128, NKC * QSP], BF16) for i in range(2)]
        wslB = [ten(s1b, f"wslb{i}", [128, NKC * QSP], BF16) for i in range(2)]
        kslab = ten(s1b, "kslab", [128, NTT * QSP], BF16)
        vtile = [ten(s1b, f"vtile{i}", [128, QSP], BF16) for i in range(2)]
        vkx = ten(s1b, "vkx", [128, 2064], F32)

        block = ctx.enter_context(nc.Block())

        def finish(sem, val, scopes, dumps=()):
            @block.sync
            def _(sync):
                sync.wait_ge(sem.h, val)
                off = 0
                for ap in dumps:
                    w = ap.shape[-1]
                    sync.dma_start(out=dbg_h[0:ap.shape[0], off:off + w], in_=ap).then_inc(s_out.h, 16)
                    s_out.v += 16
                    off += w
                sync.wait_ge(s_out.h, s_out.v)

            @block.gpsimd
            def _(g):
                g.wait_ge(s_dqst.h, s_dqst.v)
                g.wait_ge(s_cast.h, s_cast.v)
            for sc in scopes:
                sc.close()

        @block.sync
        def _(sync):
            for dst_t, src in [(eye[:, :], eye_h), (maskbd[:, :], maskbd_h), (mask4[:, :], mask4_h),
                               (mask4t[:, :], mask4t_h), (onesc[:, :], onesc_h), (onesr[:, :], onesr_h),
                               (bprow, bproj_h)]:
                s_ld.inc(sync.dma_start(out=dst_t, in_=src[:, :]), 16)
        P0_CONST = s_ld.v

        @block.gpsimd
        def _(g):
            for i in range(4):
                s_cast.inc(g.dma_start(out=xbf[i * 512:(i + 1) * 512, :],
                                       in_=x_h[i * 512:(i + 1) * 512, :]), 16)
        CAST_X = s_cast.v

        @block.vector
        def _(v):
            v.memset(cconst[:, 0:1], 252)
            v.memset(cconst[:, 1:2], 6)
            v.memset(ecol[:, 0:1], LN_EPS)
            v.memset(ecol[:, 1:2], RMS_EPS)
            v.memset(ecol[:, 2:3], ATT_EPS)
            s_vv.inc(v.engine_nop())

        @block.scalar
        def _(s):
            s.wait_ge(s_ld.h, P0_CONST)
            s_ev.inc(s.activation(out=bprow_b, in_=bprow, func=AF.Copy))
        BPROWB = s_ev.v

        @block.tensor
        def _(t):
            t.wait_ge(s_ev.h, BPROWB)
            for sp in range(4):
                s_mm.inc(t.matmul(pb[sp][:, :], onesr[:, :], bprow_b[0:1, sp * 512:(sp + 1) * 512],
                                  start=True, stop=True))
        BIAS_MM = s_mm.v

        @block.vector
        def _(v):
            v.wait_ge(s_mm.h, BIAS_MM)
            for sp in range(4):
                v.tensor_copy(bias_bc[:, sp * 512:(sp + 1) * 512], pb[sp][:, :])
            s_vv.inc(v.engine_nop())
        BIAS_BC = s_vv.v

        @block.gpsimd
        def _(g):
            for i in range(6):
                s_cast.inc(g.dma_start(out=wqkvbf[i * 1024:(i + 1) * 1024, :],
                                       in_=wqkv_h[i * 1024:(i + 1) * 1024, :]), 16)
            for i in range(2):
                s_cast.inc(g.dma_start(out=wprojbf[i * 1024:(i + 1) * 1024, :],
                                       in_=wproj_h[i * 1024:(i + 1) * 1024, :]), 16)
        CAST_QKV = [CAST_X + 16 * (i + 1) for i in range(6)]
        CAST_PROJ = s_cast.v

        @block.sync
        def _(sync):
            sync.wait_ge(s_cast.h, CAST_X)
            for kc in range(NKC):
                s_ld.inc(sync.dma_start(out=x_fm[:, kc * T:(kc + 1) * T],
                                        in_=xbf[:, kc * 128:(kc + 1) * 128],
                                        transpose=True), 16)
        XFM_DONE = s_ld.v
        if STAGE <= 1:
            finish(s_ld, XFM_DONE, [s1b, s1a],
                   dumps=[x_fm[:, 0:T].bitcast(F32), vkf[:, :]])
            return nc

        dq_work = []
        for p0 in range(11):
            for half in range(2):
                for wc in range(4):
                    dq_work.append(("gu", wc, half, p0, 1))
        for wc in range(11):
            for p0 in range(4):
                dq_work.append(("d", wc, 0, p0, 1))
        dq_items = []
        for piece in dq_work:
            dq_items.append(("slab",) + piece)
            for s_i in range(piece[4]):
                dq_items.append(("grp",) + piece + (s_i,))
        DQ_GATE_STORES = 88 * 16
        DQ_ALL_STORES = 132 * 16
        DQ_SPAN_THRESH = {}
        _cnt = 0
        for it in dq_items:
            if it[0] == "grp":
                _cnt += 16
                _, _kind, _wc, _half, _p0, _nsp, _si = it
                if _kind == "gu" and _half == 1 and _wc == 3 and _si == _nsp - 1:
                    for _spn in range(2 * _p0, 2 * (_p0 + _nsp)):
                        DQ_SPAN_THRESH[_spn] = _cnt
        dq_pos = [0]
        dq_ring = [0]
        dq_ngrp = [0]
        dq_slab_tgt = [0]

        def pump_dq(n):
            if DQOFF:
                return
            done = 0
            while done < n and dq_pos[0] < len(dq_items):
                item = dq_items[dq_pos[0]]
                if item[0] == "slab":
                    _, kind, wc, half, p0, nsp = item
                    guard = s_dqv.v

                    @block.sync
                    def _(sync, kind=kind, wc=wc, half=half, p0=p0, nsp=nsp, guard=guard):
                        if guard:
                            sync.wait_ge(s_dqv.h, guard)
                        w = nsp * 512
                        if kind == "gu":
                            r0 = half * I + p0 * 512
                            s_ld.inc(sync.dma_start(out=dq_u[:, 0:w],
                                                    in_=gup16_h[r0:r0 + w, wc * 128:(wc + 1) * 128],
                                                    transpose=True), 16)
                            s_ld.inc(sync.dma_start(out=dq_s[:, 0:w],
                                                    in_=gsrep_h[wc * 128:(wc + 1) * 128, r0:r0 + w]), 16)
                        else:
                            r0 = p0 * 512
                            s_ld.inc(sync.dma_start(out=dq_u[:, 0:w],
                                                    in_=dwn16_h[r0:r0 + w, wc * 128:(wc + 1) * 128],
                                                    transpose=True), 16)
                            s_ld.inc(sync.dma_start(out=dq_s[:, 0:w],
                                                    in_=dsrep_h[wc * 128:(wc + 1) * 128, r0:r0 + w]), 16)
                    dq_slab_tgt[0] = s_ld.v
                    dq_pos[0] += 1
                else:
                    _, kind, wc, half, p0, nsp, s_i = item
                    ring = dq_ring[0]
                    dq_ring[0] ^= 1
                    tgt_ld = dq_slab_tgt[0]
                    ngrp = dq_ngrp[0]

                    @block.vector
                    def _(v, kind=kind, wc=wc, half=half, p0=p0, s_i=s_i,
                          ring=ring, tgt_ld=tgt_ld, ngrp=ngrp):
                        ot = dq_o[ring]
                        v.wait_ge(s_ld.h, tgt_ld)
                        if ngrp >= 2:
                            v.wait_ge(s_dqst.h, (ngrp - 1) * 16)
                        off = s_i * 512
                        u = dq_u[:, off:off + 512].bitcast(U16)
                        for j in range(4):
                            if j == 0:
                                v.tensor_scalar(dq_m[:, :], u, 7, None, OP.bitwise_and)
                            else:
                                v.tensor_scalar(dq_m[:, :], u, 4 * j, 7,
                                                OP.logical_shift_right, OP.bitwise_and)
                            v.tensor_scalar(dq_t2[:, :], dq_m[:, :], 252, None, OP.mult)
                            v.scalar_tensor_tensor(dq_t2[:, :], dq_m[:, :], cconst[:, 0:1],
                                                   dq_t2[:, :], OP.add, OP.min)
                            if j == 3:
                                v.tensor_scalar(dq_sb[:, :], u, 32768, None, OP.bitwise_and)
                            else:
                                v.tensor_scalar(dq_sb[:, :], u, 12 - 4 * j, 32768,
                                                OP.logical_shift_left, OP.bitwise_and)
                            v.scalar_tensor_tensor(dq_sb[:, :], dq_t2[:, :], cconst[:, 1:2],
                                                   dq_sb[:, :], OP.logical_shift_left, OP.bitwise_or)
                            if kind == "d":
                                v.tensor_tensor(ot[:, j * 512:(j + 1) * 512],
                                                dq_sb[:, :].bitcast(BF16),
                                                dq_s[:, off:off + 512], OP.mult)
                            else:
                                oap = ot[:, j * 512:(j + 1) * 512].rearrange("p (b a) -> p b a", b=4)
                                iap = dq_sb[:, :].bitcast(BF16).rearrange("p (a b) -> p b a", a=128)
                                sap = dq_s[:, off:off + 512].rearrange("p (a b) -> p b a", a=128)
                                v.tensor_tensor(oap, iap, sap, OP.mult)
                        s_dqv.inc(v.engine_nop())
                    tgt_v = s_dqv.v

                    @block.gpsimd
                    def _(g, kind=kind, wc=wc, half=half, p0=p0, s_i=s_i, ring=ring, tgt_v=tgt_v):
                        g.wait_ge(s_dqv.h, tgt_v)
                        ot = dq_o[ring]
                        osg = p0 + s_i
                        if kind == "d":
                            ic0 = wc * 4
                            dst = wdt[ic0:ic0 + 4, :, osg * 512:(osg + 1) * 512]
                        else:
                            tgt_t = wgt if half == 0 else wup
                            dst = tgt_t[wc:16:4, :, osg * 512:(osg + 1) * 512]
                        s_dqst.inc(g.dma_start(out=dst.transpose([1, 0, 2]),
                                               in_=ot[:, :].rearrange("p (j f) -> p j f", j=4)), 16)
                    dq_ngrp[0] += 1
                    dq_pos[0] += 1
                    done += 1

        Q_MM0 = s_mm.v
        Q_EV0 = s_ev.v
        for osp in range(min(NQS, QLIM)):
            buf = wsl[osp % 2]

            @block.sync
            def _(sync, osp=osp, buf=buf):
                if osp == 0:
                    sync.wait_ge(s_ld.h, XFM_DONE)
                if osp >= 2:
                    sync.wait_ge(s_mm.h, Q_MM0 + (osp - 1) * 8)
                sync.wait_ge(s_cast.h, CAST_QKV[min(5, ((osp + 1) * QSP - 1) // 1024)])
                for kc in range(NKC):
                    s_ld.inc(sync.dma_start(out=buf[:, kc * QSP:(kc + 1) * QSP],
                                            in_=wqkvbf[osp * QSP:(osp + 1) * QSP,
                                                       kc * 128:(kc + 1) * 128],
                                            transpose=True), 16)
            Q_LD = s_ld.v

            if QMODE != "ldonly":
                @block.tensor
                def _(t, osp=osp, buf=buf, Q_LD=Q_LD):
                    t.wait_ge(s_ld.h, Q_LD)
                    if osp == 0:
                        t.wait_ge(s_vv.h, BIAS_BC)
                    for och in range(2):
                        for tsp in range(4):
                            gidx = osp * 8 + och * 4 + tsp
                            bank = pb[gidx % 2]
                            if gidx >= 2 and QMODE == "full":
                                t.wait_ge(s_ev.h, Q_EV0 + gidx - 1)
                            for kc in range(NKC):
                                mi = t.matmul(bank[:, :],
                                              buf[:, kc * QSP + och * 128: kc * QSP + (och + 1) * 128],
                                              x_fm[:, kc * T + tsp * 512: kc * T + (tsp + 1) * 512],
                                              start=(kc == 0), stop=(kc == NKC - 1))
                            s_mm.inc(mi)

            if QMODE == "full":
                @block.scalar
                def _(s, osp=osp):
                    for och in range(2):
                        for tsp in range(4):
                            gidx = osp * 8 + och * 4 + tsp
                            bank = pb[gidx % 2]
                            s.wait_ge(s_mm.h, Q_MM0 + gidx + 1)
                            oc = osp * 2 + och
                            s_ev.inc(s.activation(
                                out=q_fm[:, oc * T + tsp * 512: oc * T + (tsp + 1) * 512],
                                in_=bank[:, :], func=AF.Relu))
            pump_dq(10)
        Q_PE_DONE = Q_MM0 + min(NQS, QLIM) * 8
        if STAGE == 12:
            @block.scalar
            def _(s):
                pass
            finish(s_ev, s_ev.v, [s1b, s1a])
            return nc

        kv_mm = {}
        kv_ev = {}
        KV_MM0 = s_mm.v
        for sp in range(min(NQS, KVLIM)):
            bufk = wsl[0] if sp % 2 == 0 else wslB[0]
            bufv = wsl[1] if sp % 2 == 0 else wslB[1]
            vk_bank = pb[4 + (sp % 2)]

            @block.sync
            def _(sync, sp=sp, bufk=bufk, bufv=bufv):
                if sp == 0:
                    sync.wait_ge(s_cast.h, CAST_QKV[5])
                    sync.wait_ge(s_mm.h, Q_PE_DONE)
                elif sp == 1:
                    sync.wait_ge(s_cast.h, CAST_QKV[5])
                else:
                    sync.wait_ge(s_mm.h, kv_mm[(sp - 2, "pe_done")])
                for kc in range(NKC):
                    s_ld.inc(sync.dma_start(out=bufk[:, kc * QSP:(kc + 1) * QSP],
                                            in_=wqkvbf[2048 + sp * QSP: 2048 + (sp + 1) * QSP,
                                                       kc * 128:(kc + 1) * 128],
                                            transpose=True), 16)
                for kc in range(NKC):
                    s_ld.inc(sync.dma_start(out=bufv[:, kc * QSP:(kc + 1) * QSP],
                                            in_=wqkvbf[4096 + sp * QSP: 4096 + (sp + 1) * QSP,
                                                       kc * 128:(kc + 1) * 128],
                                            transpose=True), 16)
            kv_mm[(sp, "wld")] = s_ld.v
            if STAGE == 17 and sp == 0:
                finish(s_ld, kv_mm[(0, "wld")], [s1b, s1a],
                       dumps=[bufk[:, :].bitcast(F32), bufv[:, :].bitcast(F32)])
                return nc
            ev0 = s_ev.v
            kv_ev[sp] = ev0
            mm0 = s_mm.v
            kv_mm[(sp, "mm0")] = mm0

            @block.tensor
            def _(t, sp=sp, bufk=bufk, bufv=bufv, vk_bank=vk_bank, ev0=ev0, mm0=mm0):
                t.wait_ge(s_ld.h, kv_mm[(sp, "wld")])
                if sp >= 1:
                    t.wait_ge(s_ev.h, kv_mm[(sp - 1, "ev_done")])
                if sp >= 2:
                    t.wait_ge(s_vv.h, kv_mm[(sp - 2, "vkev")])
                for tt in range(TTLIM):
                    if tt >= 1:
                        t.wait_ge(s_ev.h, ev0 + 2 * (tt - 1) + 1)
                    for kc in range(NKC):
                        mi = t.matmul(pb[2][:, 0:QSP],
                                      x_fm[:, kc * T + tt * 128: kc * T + (tt + 1) * 128],
                                      bufk[:, kc * QSP:(kc + 1) * QSP],
                                      start=(kc == 0), stop=(kc == NKC - 1))
                    s_mm.inc(mi)
                    if tt >= 1:
                        t.wait_ge(s_ev.h, ev0 + 2 * (tt - 1) + 2)
                    for kc in range(NKC):
                        mi = t.matmul(pb[3][:, 0:QSP],
                                      x_fm[:, kc * T + tt * 128: kc * T + (tt + 1) * 128],
                                      bufv[:, kc * QSP:(kc + 1) * QSP],
                                      start=(kc == 0), stop=(kc == NKC - 1))
                    s_mm.inc(mi)
                    if os.environ.get("KNOVK", "") == "1":
                        s_mm.inc(mi)
                        continue
                    t.wait_ge(s_ev.h, ev0 + 2 * tt + 2)
                    vb = vtile[tt % 2]
                    for cl in range(2):
                        t.matmul(vk_bank[:, cl * 128:(cl + 1) * 128],
                                 vb[:, cl * 128:(cl + 1) * 128],
                                 kslab[:, tt * QSP + cl * 128: tt * QSP + (cl + 1) * 128],
                                 start=(tt == 0 and cl == 0), stop=(tt == TTLIM - 1 and cl == 1))
                        mi = t.matmul(pks[:, (sp * 2 + cl):(sp * 2 + cl) + 1],
                                      kslab[:, tt * QSP + cl * 128: tt * QSP + (cl + 1) * 128],
                                      onesc[:, :],
                                      start=(sp == 0 and tt == 0 and cl == 0),
                                      stop=(sp == min(NQS, KVLIM) - 1 and tt == TTLIM - 1 and cl == 1))
                    s_mm.inc(mi)
            kv_mm[(sp, "pe_done")] = s_mm.v

            @block.scalar
            def _(s, sp=sp, mm0=mm0):
                if os.environ.get("KNOKEV", "") == "1":
                    return
                for tt in range(TTLIM):
                    s.wait_ge(s_mm.h, mm0 + 3 * tt + 1)
                    if tt >= 2:
                        s.wait_ge(s_mm.h, mm0 + 3 * (tt - 2) + 3)
                    s_ev.inc(s.activation(out=kslab[:, tt * QSP:(tt + 1) * QSP],
                                          in_=pb[2][:, 0:QSP], func=AF.Relu))
                    s.wait_ge(s_mm.h, mm0 + 3 * tt + 2)
                    s_ev.inc(s.activation(out=vtile[tt % 2][:, :],
                                          in_=pb[3][:, 0:QSP], func=AF.Copy))
            kv_mm[(sp, "ev_done")] = s_ev.v

            @block.vector
            def _(v, sp=sp, vk_bank=vk_bank):
                v.wait_ge(s_mm.h, kv_mm[(sp, "pe_done")])
                v.tensor_copy(vkx[:, sp * QSP:(sp + 1) * QSP], vk_bank[:, 0:QSP])
                s_vv.inc(v.engine_nop())
            kv_mm[(sp, "vkev")] = s_vv.v
            pump_dq(13)

        @block.vector
        def _(v):
            v.wait_ge(s_mm.h, kv_mm[(min(NQS, KVLIM) - 1, "pe_done")])
            v.tensor_copy(vkx[:, 2048:2064], pks[:, :])
            s_vv.inc(v.engine_nop())
        VKX_DONE = s_vv.v
        if STAGE <= 2:
            finish(s_vv, VKX_DONE, [s1b, s1a],
                   dumps=() if os.environ.get("KNODMP", "") == "1" else (
                          vkx[:, :],
                          q_fm[:, 0:T].bitcast(F32),
                          x_fm[:, 0:T].bitcast(F32),
                          kslab[:, :].bitcast(F32)))
            return nc

        @block.scalar
        def _(s):
            s.wait_ge(s_vv.h, VKX_DONE)
            s_st.inc(s.dma_start(out=cin[:, :], in_=vkx[:, :]), 16)
        CIN_ST = s_st.v

        @block.gpsimd
        def _(g):
            g.wait_ge(s_st.h, CIN_ST)
            s_cc.inc(g.collective_compute(
                "AllReduce", OP.add,
                replica_groups=[[0, 1], [2, 3], [4, 5], [6, 7]],
                ins=[cin[:, :].opt()],
                outs=[cfull[:, :].opt()]), 1)

        @block.sync
        def _(sync):
            sync.wait_ge(s_cc.h, s_cc.v)
            s_ld.inc(sync.dma_start(out=vkf[:, :], in_=cfull[:, :]), 16)
        VKF_LD = s_ld.v
        if STAGE <= 3:
            finish(s_ld, VKF_LD, [s1b, s1a])
            return nc

        pump_dq(8)
        s1b.close()
        s2 = ExitStack()
        wpT = ten(s2, "wpT", [128, NKC * 2048], BF16)
        rden = ten(s2, "rden", [128, 2048], BF16)
        xres = [ten(s2, f"xres{i}", [128, D], F32) for i in range(2)]
        rbuf = ten(s2, "rbuf", [128, D], F32)
        x1b = [ten(s2, "x1b0", [128, D], BF16)] * 2
        tmp128 = ten(s2, "tmp128", [128, 128], BF16)
        Abuf = [ten(s2, f"Abuf{i}", [128, 128], BF16) for i in range(2)]
        cbuf = ten(s2, "cbuf", [128, 4], BF16)
        den4b = ten(s2, "den4b", [4, 2048], BF16)

        @block.sync
        def _(sync):
            sync.wait_ge(s_cast.h, CAST_PROJ)
            sync.wait_ge(s_mm.h, kv_mm[(NQS - 1, "pe_done")])
            sync.wait_ge(s_st.h, CIN_ST)
            for kc in range(NKC):
                for osp in range(4):
                    s_ld.inc(sync.dma_start(
                        out=wpT[:, kc * 2048 + osp * 512: kc * 2048 + (osp + 1) * 512],
                        in_=wprojbf[osp * 512:(osp + 1) * 512, kc * 128:(kc + 1) * 128],
                        transpose=True), 16)
        WPT_DONE = s_ld.v

        ph4 = {}
        for c in range(NKC):
            @block.vector
            def _(v, c=c):
                if c == 0:
                    v.wait_ge(s_ld.h, VKF_LD)
                if c >= 1:
                    v.wait_ge(s_mm.h, ph4[(c - 1, "denmm", 3)])
                v.tensor_tensor(tmp128[:, :], vkf[:, c * 128:(c + 1) * 128], maskbd[:, :], OP.mult)
                v.tensor_scalar(cbuf[:, :], mask4[:, :], vkf[:, 2048 + c:2049 + c], None, OP.mult)
                s_vv.inc(v.engine_nop())
            ph4[(c, "prep")] = s_vv.v

            @block.tensor
            def _(t, c=c):
                t.wait_ge(s_vv.h, ph4[(c, "prep")])
                if c >= 1:
                    t.wait_ge(s_ev.h, ph4[(c - 1, "Aev")])
                s_mm.inc(t.transpose(pT[:, :], tmp128[:, :], eye[:, :]))
            ph4[(c, "tr")] = s_mm.v

            @block.scalar
            def _(s, c=c):
                s.wait_ge(s_mm.h, ph4[(c, "tr")])
                s_ev.inc(s.activation(out=Abuf[c % 2][:, :], in_=pT[:, :], func=AF.Copy))
            ph4[(c, "Aev")] = s_ev.v

            den_mm = []
            den_ev = []
            dbc_mm = []
            dbc_ev = []

            DEN_PRED = s_vv.v

            @block.tensor
            def _(t, c=c, DEN_PRED=DEN_PRED):
                t.wait_ge(s_ev.h, ph4[(c, "Aev")])
                for tsp in range(4):
                    bank = pb[2 + tsp % 2]
                    if tsp >= 2:
                        t.wait_ge(s_vv.h, DEN_PRED + tsp - 1)
                    elif c >= 1:
                        t.wait_ge(s_ev.h, ph4[(c - 1, "dbcev", 2 + tsp)])
                    s_mm.inc(t.matmul(bank[0:4, :], cbuf[:, :],
                             q_fm[:, c * T + tsp * 512: c * T + (tsp + 1) * 512],
                             start=True, stop=True))
                    ph4[(c, "denmm", tsp)] = s_mm.v

            @block.vector
            def _(v, c=c):
                for tsp in range(4):
                    bank = pb[2 + tsp % 2]
                    v.wait_ge(s_mm.h, ph4[(c, "denmm", tsp)])
                    v.tensor_scalar(den4b[:, tsp * 512:(tsp + 1) * 512], bank[0:4, :],
                                    ecol[0:4, 2:3], None, OP.add)
                    s_vv.inc(v.engine_nop())
                    ph4[(c, "denev", tsp)] = s_vv.v

            OUTEV_PRED = s_vv.v

            @block.tensor
            def _(t, c=c, OUTEV_PRED=OUTEV_PRED):
                for tsp in range(4):
                    bank = pb[2 + tsp % 2]
                    t.wait_ge(s_vv.h, ph4[(c, "denev", tsp + 2 if tsp < 2 else 3)])
                    if tsp >= 2:
                        t.wait_ge(s_ev.h, ph4[(c, "Aev")] + 2 * (tsp - 1))
                    s_mm.inc(t.matmul(bank[:, :], mask4t[:, :],
                             den4b[:, tsp * 512:(tsp + 1) * 512], start=True, stop=True))
                    ph4[(c, "dbcmm", tsp)] = s_mm.v
                for tsp in range(4):
                    bank = pb[tsp % 2]
                    if tsp >= 2:
                        t.wait_ge(s_vv.h, OUTEV_PRED + tsp - 1)
                    elif c >= 1:
                        t.wait_ge(s_vv.h, ph4[(c - 1, "outev", tsp + 2)])
                    s_mm.inc(t.matmul(bank[:, :], Abuf[c % 2][:, :],
                             q_fm[:, c * T + tsp * 512: c * T + (tsp + 1) * 512],
                             start=True, stop=True))
                    ph4[(c, "outmm", tsp)] = s_mm.v

            @block.scalar
            def _(s, c=c):
                for tsp in range(4):
                    bank = pb[2 + tsp % 2]
                    s.wait_ge(s_mm.h, ph4[(c, "dbcmm", tsp)])
                    s_ev.inc(s.activation(out=rden[:, tsp * 512:(tsp + 1) * 512],
                                          in_=bank[:, :], func=AF.Ln))
                    s_ev.inc(s.activation(out=rden[:, tsp * 512:(tsp + 1) * 512],
                                          in_=rden[:, tsp * 512:(tsp + 1) * 512],
                                          func=AF.Exp, scale=-1.0))
                    ph4[(c, "dbcev", tsp)] = s_ev.v

            @block.vector
            def _(v, c=c):
                for tsp in range(4):
                    bank = pb[tsp % 2]
                    v.wait_ge(s_mm.h, ph4[(c, "outmm", tsp)])
                    v.wait_ge(s_ev.h, ph4[(c, "dbcev", tsp)])
                    v.tensor_tensor(q_fm[:, c * T + tsp * 512: c * T + (tsp + 1) * 512],
                                    bank[:, :], rden[:, tsp * 512:(tsp + 1) * 512], OP.mult)
                    s_vv.inc(v.engine_nop())
                    ph4[(c, "outev", tsp)] = s_vv.v
            if c % 2 == 1:
                pump_dq(1)
        P4_DONE_VV = s_vv.v
        if STAGE <= 4:
            finish(s_vv, P4_DONE_VV, [s2, s1a],
                   dumps=[q_fm[:, 0:T].bitcast(F32), rden[:, :].bitcast(F32), vkf[:, :]])
            return nc

        ph5 = {}
        for tt in range(NTT):
            @block.sync
            def _(sync, tt=tt):
                if tt == 0:
                    sync.wait_ge(s_ld.h, WPT_DONE)
                    sync.wait_ge(s_vv.h, P4_DONE_VV)
                if tt >= 2:
                    sync.wait_ge(s_vv.h, ph5[(tt - 2, "rsum")])
                s_ld.inc(sync.dma_start(out=xres[tt % 2][:, :],
                                        in_=x_h[tt * 128:(tt + 1) * 128, :]), 16)
            ph5[(tt, "xld")] = s_ld.v

            @block.vector
            def _(v, tt=tt):
                v.wait_ge(s_ld.h, ph5[(tt, "xld")])
                v.tensor_tensor(xres[tt % 2][:, :], xres[tt % 2][:, :], bias_bc[:, :], OP.add)
                s_vv.inc(v.engine_nop())
                ph5[(tt, "xb")] = s_vv.v

            @block.tensor
            def _(t, tt=tt):
                if tt == 0:
                    t.wait_ge(s_vv.h, P4_DONE_VV)
                    t.wait_ge(s_ev.h, ph4[(NKC - 1, "dbcev", 3)])
                for osp in range(4):
                    gidx = tt * 4 + osp
                    bank = pb[gidx % 4]
                    if gidx >= 4:
                        t.wait_ge(s_vv.h, ph5[(gidx - 4, "prjev")])
                    for kc in range(NKC):
                        mi = t.matmul(bank[:, :],
                                      q_fm[:, kc * T + tt * 128: kc * T + (tt + 1) * 128],
                                      wpT[:, kc * 2048 + osp * 512: kc * 2048 + (osp + 1) * 512],
                                      start=(kc == 0), stop=(kc == NKC - 1))
                    s_mm.inc(mi)
                    ph5[(gidx, "prjmm")] = s_mm.v

            @block.vector
            def _(v, tt=tt):
                v.wait_ge(s_vv.h, ph5[(tt, "xb")])
                if tt >= 1:
                    v.wait_ge(s_st.h, ph5[(tt - 1, "x1st")])
                    v.wait_ge(s_ev.h, ph5[(tt - 1, "x1b")])
                for osp in range(4):
                    gidx = tt * 4 + osp
                    bank = pb[gidx % 4]
                    v.wait_ge(s_mm.h, ph5[(gidx, "prjmm")])
                    v.scalar_tensor_tensor(rbuf[:, osp * 512:(osp + 1) * 512],
                                           bank[:, :], 1.0,
                                           xres[tt % 2][:, osp * 512:(osp + 1) * 512],
                                           OP.bypass, OP.add)
                    s_vv.inc(v.engine_nop())
                    ph5[(gidx, "prjev")] = s_vv.v
                ph5[(tt, "rsum")] = s_vv.v

            @block.scalar
            def _(s, tt=tt):
                s.wait_ge(s_vv.h, ph5[(tt, "rsum")])
                s.activation(out=rden[:, :], in_=rbuf[:, :], func=AF.Identity,
                             scale=1.0 / D, accum_out=stats[:, 2:3])
                s.activation(out=rden[:, :], in_=rbuf[:, :], func=AF.Square,
                             scale=1.0 / math.sqrt(D), accum_out=stats[:, 3:4])
                s_ev.inc(s.drain())
                ph5[(tt, "sq")] = s_ev.v

            @block.vector
            def _(v, tt=tt):
                v.wait_ge(s_ev.h, ph5[(tt, "sq")])
                v.scalar_tensor_tensor(stats[:, 5:6], stats[:, 2:3], stats[:, 2:3],
                                       stats[:, 3:4], OP.mult, OP.subtract)
                s_vv.inc(v.engine_nop())
                ph5[(tt, "var")] = s_vv.v

            @block.scalar
            def _(s, tt=tt):
                s.wait_ge(s_vv.h, ph5[(tt, "var")])
                s.activation(out=stats[:, 6:7], in_=stats[:, 5:6], func=AF.Ln,
                             bias=ecol[:, 0:1], scale=-1.0)
                s_ev.inc(s.activation(out=stats[:, 7:8], in_=stats[:, 6:7], func=AF.Exp,
                                      scale=-0.5))
                ph5[(tt, "rstd")] = s_ev.v

            @block.vector
            def _(v, tt=tt):
                v.wait_ge(s_ev.h, ph5[(tt, "rstd")])
                v.tensor_scalar(rbuf[:, :], rbuf[:, :], stats[:, 2:3], stats[:, 7:8],
                                OP.subtract, OP.mult)
                s_vv.inc(v.engine_nop())
                ph5[(tt, "x1")] = s_vv.v

            @block.scalar
            def _(s, tt=tt):
                s.wait_ge(s_vv.h, ph5[(tt, "x1")])
                if tt >= 1:
                    s.wait_ge(s_st.h, ph5[(tt - 1, "x1st")])
                for j in range(4):
                    ai = s.activation(out=x1b[tt % 2][:, j * 512:(j + 1) * 512],
                                      in_=rbuf[:, j:D:4], func=AF.Copy)
                s_ev.inc(ai)
                ph5[(tt, "x1b")] = s_ev.v
                s_st.inc(s.dma_start(out=x1f[tt * 128:(tt + 1) * 128, :], in_=rbuf[:, :]), 16)
                s.wait_ge(s_ev.h, ph5[(tt, "x1b")])
                s_st.inc(s.dma_start(out=x1bf[tt * 128:(tt + 1) * 128, :], in_=x1b[tt % 2][:, :]), 16)
            ph5[(tt, "x1st")] = s_st.v

            pump_dq(3)
        P5_PE_DONE = ph5[(NTT * 4 - 1, "prjmm")]
        P5_ST_DONE = ph5[(NTT - 1, "x1st")]
        P5_EV_DONE = ph5[(NTT - 1, "x1b")]
        P5_VV_DONE = s_vv.v
        if STAGE <= 5:
            finish(s_st, P5_ST_DONE, [s2, s1a],
                   dumps=[rbuf[:, :], stats[:, :]])
            return nc

        s2.close()
        s1a.close()
        s3 = ExitStack()
        x1fm = ten(s3, "x1fm", [128, NKC * FTC], BF16)
        wgu = [ten(s3, f"wgu{i}", [128, NKC * 256], BF16) for i in range(2)]
        wub = [ten(s3, f"wub{i}", [128, NKC * 256], BF16) for i in range(2)]
        hbuf = ten(s3, "hbuf", [128, 44 * FTC], BF16)
        sgt = ten(s3, "sgt", [128, FTC], BF16)
        wdsl = [ten(s3, f"wdsl{i}", [128, 11 * 512], BF16) for i in range(2)]
        y2 = [ten(s3, f"y2_{i}", [128, D], F32) for i in range(2)]
        xr2 = [ten(s3, "xr2_0", [128, D], F32)] * 2
        junk3 = ten(s3, "junk3", [128, D], BF16)

        pump_dq(10 ** 9)

        ffn = {}
        S3_GUARD_EMITTED = [False]
        for tc in range(NFT):
            @block.sync
            def _(sync, tc=tc):
                if not S3_GUARD_EMITTED[0]:
                    sync.wait_ge(s_mm.h, P5_PE_DONE)
                    sync.wait_ge(s_st.h, P5_ST_DONE)
                    sync.wait_ge(s_ev.h, P5_EV_DONE)
                    sync.wait_ge(s_vv.h, P5_VV_DONE)
                    S3_GUARD_EMITTED[0] = True
                sync.wait_ge(s_st.h, ph5[(tc * 4 + 3, "x1st")])
                if tc >= 1:
                    sync.wait_ge(s_mm.h, ffn[(tc - 1, "gu_pe")])
                for kc in range(NKC):
                    s_ld.inc(sync.dma_start(out=x1fm[:, kc * FTC:(kc + 1) * FTC],
                                            in_=x1bf[tc * FTC:(tc + 1) * FTC,
                                                     kc * 128:(kc + 1) * 128],
                                            transpose=True), 16)
            ffn[(tc, "x1fm")] = s_ld.v

            GSPN = 22
            for spn in range(GSPN):
                @block.sync
                def _(sync, tc=tc, spn=spn):
                    gi = tc * GSPN + spn
                    if tc == 0 and not DQOFF:
                        sync.wait_ge(s_dqst.h, DQ_SPAN_THRESH[spn])
                    if gi >= 2:
                        sync.wait_ge(s_mm.h, ffn[(gi - 2, "gu_sp_pe")])
                    ring = spn % 2
                    s_ld.inc(sync.dma_start(
                        out=wgu[ring][:, :].rearrange("p (k f) -> p k f", k=16),
                        in_=wgt[0:16, :, spn * 256:(spn + 1) * 256].transpose([1, 0, 2])), 16)
                    s_ld.inc(sync.dma_start(
                        out=wub[ring][:, :].rearrange("p (k f) -> p k f", k=16),
                        in_=wup[0:16, :, spn * 256:(spn + 1) * 256].transpose([1, 0, 2])), 16)
                ffn[(tc * GSPN + spn, "gu_ld")] = s_ld.v

                @block.tensor
                def _(t, tc=tc, spn=spn):
                    gi = tc * GSPN + spn
                    t.wait_ge(s_ld.h, ffn[(gi, "gu_ld")])
                    if spn == 0:
                        t.wait_ge(s_ld.h, ffn[(tc, "x1fm")])
                    ring = spn % 2
                    for ol in range(2):
                        ic = spn * 2 + ol
                        bg = pb[0 if ic % 2 == 0 else 2]
                        bu = pb[1 if ic % 2 == 0 else 3]
                        if ic >= 2:
                            t.wait_ge(s_vv.h, ffn[(tc, "h", ic - 2)])
                        for kc in range(NKC):
                            mi = t.matmul(bg[:, :],
                                          wgu[ring][:, kc * 256 + ol * 128: kc * 256 + (ol + 1) * 128],
                                          x1fm[:, kc * FTC:(kc + 1) * FTC],
                                          start=(kc == 0), stop=(kc == NKC - 1))
                        s_mm.inc(mi)
                        ffn[(tc, "gmm", ic)] = s_mm.v
                        for kc in range(NKC):
                            mi = t.matmul(bu[:, :],
                                          wub[ring][:, kc * 256 + ol * 128: kc * 256 + (ol + 1) * 128],
                                          x1fm[:, kc * FTC:(kc + 1) * FTC],
                                          start=(kc == 0), stop=(kc == NKC - 1))
                        s_mm.inc(mi)
                        ffn[(tc, "umm", ic)] = s_mm.v
                ffn[(tc * GSPN + spn, "gu_sp_pe")] = s_mm.v

                H_PRED = s_vv.v

                @block.scalar
                def _(s, tc=tc, spn=spn, H_PRED=H_PRED):
                    for ol in range(2):
                        ic = spn * 2 + ol
                        bg = pb[0 if ic % 2 == 0 else 2]
                        s.wait_ge(s_mm.h, ffn[(tc, "gmm", ic)])
                        if ol == 1:
                            s.wait_ge(s_vv.h, H_PRED + 1)
                        elif ic >= 1:
                            s.wait_ge(s_vv.h, ffn[(tc, "h", ic - 1)])
                        s_ev.inc(s.activation(out=sgt[:, :], in_=bg[:, :], func=AF.Silu))
                        ffn[(tc, "sg", ic)] = s_ev.v

                @block.vector
                def _(v, tc=tc, spn=spn):
                    for ol in range(2):
                        ic = spn * 2 + ol
                        bu = pb[1 if ic % 2 == 0 else 3]
                        if tc >= 1 and spn == 0 and ol == 0:
                            v.wait_ge(s_mm.h, ffn[("dnmm", (tc * 2 - 1) * 4 + 3)])
                        v.wait_ge(s_ev.h, ffn[(tc, "sg", ic)])
                        v.wait_ge(s_mm.h, ffn[(tc, "umm", ic)])
                        v.tensor_tensor(hbuf[:, ic * FTC:(ic + 1) * FTC],
                                        sgt[:, :], bu[:, :], OP.mult)
                        s_vv.inc(v.engine_nop())
                        ffn[(tc, "h", ic)] = s_vv.v
            ffn[(tc, "gu_pe")] = s_mm.v
            if STAGE == 21 and tc == 0:
                if os.environ.get("KDSEL", "0") == "1":
                    finish(s_vv, ffn[(0, "h", 43)], [s3],
                           dumps=[wgu[0][:, :].bitcast(F32), wgu[1][:, :].bitcast(F32),
                                  x1fm[:, 0:8192].bitcast(F32)])
                else:
                    finish(s_vv, ffn[(0, "h", 43)], [s3],
                           dumps=[hbuf[:, 0:16384].bitcast(F32)])
                return nc

            for pr in range(2):
                pg = tc * 2 + pr
                for osp in range(4):
                    di = pg * 4 + osp

                    DN_PRED = s_mm.v

                    @block.sync
                    def _(sync, osp=osp, di=di, DN_PRED=DN_PRED):
                        if di == 0:
                            sync.wait_ge(s_dqst.h, DQ_ALL_STORES)
                        for qq in range(4):
                            if qq >= 2:
                                sync.wait_ge(s_mm.h, DN_PRED + qq - 1)
                            elif di >= 1:
                                sync.wait_ge(s_mm.h, ffn[("dn_q", di - 1, qq + 2)])
                            s_ld.inc(sync.dma_start(
                                out=wdsl[qq % 2][:, :].rearrange("p (k f) -> p k f", k=11),
                                in_=wdt[qq * 11:(qq + 1) * 11, :,
                                        osp * 512:(osp + 1) * 512].transpose([1, 0, 2])), 16)
                            ffn[("wd_ld", di, qq)] = s_ld.v

                    @block.tensor
                    def _(t, tc=tc, pr=pr, osp=osp, di=di):
                        for qq in range(4):
                            t.wait_ge(s_ld.h, ffn[("wd_ld", di, qq)])
                            t.wait_ge(s_vv.h, ffn[(tc, "h", qq * 11 + 10)])
                            if qq == 0 and di >= 1:
                                for ttl in range(2):
                                    t.wait_ge(s_ev.h, ffn[("yev", di - 1, ttl)])
                            for il in range(11):
                                ic = qq * 11 + il
                                for ttl in range(2):
                                    tl = pr * 2 + ttl
                                    bank = pb[4 + ttl]
                                    mi = t.matmul(bank[:, :],
                                                  hbuf[:, ic * FTC + tl * 128: ic * FTC + tl * 128 + 128],
                                                  wdsl[qq % 2][:, il * 512:(il + 1) * 512],
                                                  start=(ic == 0), stop=(ic == 43))
                            s_mm.inc(mi)
                            ffn[("dn_q", di, qq)] = s_mm.v
                        ffn[("dnmm", di)] = s_mm.v

                    @block.scalar
                    def _(s, pg=pg, osp=osp, di=di):
                        s.wait_ge(s_mm.h, ffn[("dnmm", di)])
                        for ttl in range(2):
                            bank = pb[4 + ttl]
                            if pg >= 1:
                                s.wait_ge(s_out.h, ffn[("ost", (pg - 1) * 2 + ttl)])
                            s_ev.inc(s.activation(out=y2[ttl][:, osp * 512:(osp + 1) * 512],
                                                  in_=bank[:, :], func=AF.Copy))
                            ffn[("yev", di, ttl)] = s_ev.v

                for ttl in range(2):
                    tt = tc * 4 + pr * 2 + ttl

                    @block.sync
                    def _(sync, tt=tt, ttl=ttl, tc=tc, pr=pr):
                        sync.wait_ge(s_st.h, ph5[(tt, "x1st")])
                        if tt >= 1:
                            sync.wait_ge(s_vv.h, ffn.get(("r2", tt - 1), 0))
                        s_ld.inc(sync.dma_start(out=xr2[ttl][:, :],
                                                in_=x1f[tt * 128:(tt + 1) * 128, :]), 16)
                        ffn[("xr2", tt)] = s_ld.v

                    @block.scalar
                    def _(s, tt=tt, ttl=ttl, tc=tc, pr=pr):
                        di_last = (tc * 2 + pr) * 4 + 3
                        s.wait_ge(s_ev.h, ffn[("yev", di_last, ttl)])
                        if tt == 15 and YDUMP:
                            s_st.inc(s.dma_start(out=dbg_h[:, 0:2048], in_=y2[ttl][:, :]), 16)
                            ffn[("ydmp", tt)] = s_st.v
                        s.activation(out=junk3[:, :], in_=y2[ttl][:, :], func=AF.Square,
                                     scale=1.0 / math.sqrt(D), accum_out=stats[:, 3:4])
                        s_ev.inc(s.drain())
                        ffn[("ysq", tt)] = s_ev.v

                    @block.vector
                    def _(v, tt=tt):
                        v.wait_ge(s_ev.h, ffn[("ysq", tt)])
                        v.tensor_copy(stats[:, 14:15], stats[:, 3:4])
                        s_vv.inc(v.engine_nop())
                        ffn[("msq", tt)] = s_vv.v

                    @block.scalar
                    def _(s, tt=tt):
                        s.wait_ge(s_vv.h, ffn[("msq", tt)])
                        s.activation(out=stats[:, 6:7], in_=stats[:, 14:15], func=AF.Ln,
                                     bias=ecol[:, 1:2], scale=1.0)
                        s_ev.inc(s.activation(out=stats[:, 7:8], in_=stats[:, 6:7], func=AF.Exp,
                                              scale=-0.5))
                        ffn[("rrms", tt)] = s_ev.v

                    @block.vector
                    def _(v, tt=tt, ttl=ttl):
                        v.wait_ge(s_ev.h, ffn[("rrms", tt)])
                        v.wait_ge(s_ld.h, ffn[("xr2", tt)])
                        if tt == 15 and YDUMP:
                            v.wait_ge(s_st.h, ffn[("ydmp", tt)])
                        for osp in range(4):
                            v.scalar_tensor_tensor(y2[ttl][:, osp * 512:(osp + 1) * 512],
                                                   y2[ttl][:, osp * 512:(osp + 1) * 512],
                                                   stats[:, 7:8],
                                                   xr2[ttl][:, osp * 512:(osp + 1) * 512],
                                                   OP.mult, OP.add)
                        s_vv.inc(v.engine_nop())
                        ffn[("r2", tt)] = s_vv.v

                    @block.scalar
                    def _(s, tt=tt, ttl=ttl):
                        s.wait_ge(s_vv.h, ffn[("r2", tt)])
                        s.activation(out=junk3[:, :], in_=y2[ttl][:, :], func=AF.Identity,
                                     scale=1.0 / D, accum_out=stats[:, 2:3])
                        s.activation(out=junk3[:, :], in_=y2[ttl][:, :], func=AF.Square,
                                     scale=1.0 / math.sqrt(D), accum_out=stats[:, 3:4])
                        s_ev.inc(s.drain())
                        ffn[("r2sq", tt)] = s_ev.v

                    @block.vector
                    def _(v, tt=tt):
                        v.wait_ge(s_ev.h, ffn[("r2sq", tt)])
                        v.scalar_tensor_tensor(stats[:, 5:6], stats[:, 2:3], stats[:, 2:3],
                                               stats[:, 3:4], OP.mult, OP.subtract)
                        s_vv.inc(v.engine_nop())
                        ffn[("var2", tt)] = s_vv.v

                    @block.scalar
                    def _(s, tt=tt):
                        s.wait_ge(s_vv.h, ffn[("var2", tt)])
                        s.activation(out=stats[:, 6:7], in_=stats[:, 5:6], func=AF.Ln,
                                     bias=ecol[:, 0:1], scale=-1.0)
                        s_ev.inc(s.activation(out=stats[:, 7:8], in_=stats[:, 6:7], func=AF.Exp,
                                              scale=-0.5))
                        ffn[("rstd2", tt)] = s_ev.v
                        if tt == 15 and YDUMP:
                            s.wait_ge(s_ev.h, ffn[("rstd2", tt)])
                            s_st.inc(s.dma_start(out=dbg_h[:, 2048:2064], in_=stats[:, :]), 16)

                    @block.vector
                    def _(v, tt=tt, ttl=ttl):
                        v.wait_ge(s_ev.h, ffn[("rstd2", tt)])
                        v.tensor_scalar(y2[ttl][:, :], y2[ttl][:, :], stats[:, 2:3], stats[:, 7:8],
                                        OP.subtract, OP.mult)
                        s_vv.inc(v.engine_nop())
                        ffn[("o", tt)] = s_vv.v

                    @block.scalar
                    def _(s, tt=tt, ttl=ttl):
                        s.wait_ge(s_vv.h, ffn[("o", tt)])
                        s_out.inc(s.dma_start(out=out_h[tt * 128:(tt + 1) * 128, :],
                                              in_=y2[ttl][:, :]), 16)
                    ffn[("ost", tt)] = s_out.v

        OST_ALL = s_out.v

        @block.sync
        def _(sync):
            sync.wait_ge(s_out.h, OST_ALL)
            if YDUMP:
                s_out.inc(sync.dma_start(out=dbg_h[:, 2064:4112],
                                         in_=hbuf[:, 0:4096].bitcast(F32)), 16)
                s_out.inc(sync.dma_start(out=dbg_h[:, 4112:6160],
                                         in_=x1fm[:, 0:4096].bitcast(F32)), 16)
                sync.wait_ge(s_out.h, s_out.v)
        s3.close()
    return nc


def _host_prep(inputs):
    x = np.ascontiguousarray(np.asarray(inputs["L0_attn_x"], dtype=np.float32)).reshape(16384, D)
    wqkv = np.ascontiguousarray(np.asarray(inputs["L0_attn_w_qkv"], dtype=np.float32))
    wproj = np.ascontiguousarray(np.asarray(inputs["L0_attn_w_proj"], dtype=np.float32))
    bproj = np.ascontiguousarray(np.asarray(inputs["L0_attn_b_proj"], dtype=np.float32)).reshape(1, D)
    gup = np.ascontiguousarray(np.asarray(inputs["L0_ffn_gate_up_packed"], dtype=np.int32))
    gus = np.ascontiguousarray(np.asarray(inputs["L0_ffn_gate_up_scales"], dtype=np.float16))
    dwn = np.ascontiguousarray(np.asarray(inputs["L0_ffn_down_packed"], dtype=np.int32))
    dws = np.ascontiguousarray(np.asarray(inputs["L0_ffn_down_scales"], dtype=np.float16))
    gup16 = gup.view(np.float16)
    dwn16 = dwn.view(np.float16)
    gsrep = np.ascontiguousarray(np.repeat(gus, 32, axis=0))
    dsrep = np.ascontiguousarray(np.repeat(dws, 32, axis=0))
    eye = np.eye(128, dtype=ml_dtypes.bfloat16)
    blk = (np.arange(128)[:, None] // 32 == np.arange(128)[None, :] // 32)
    maskbd = blk.astype(ml_dtypes.bfloat16)
    m4 = (np.arange(128)[:, None] // 32 == np.arange(4)[None, :]).astype(ml_dtypes.bfloat16)
    m4t = np.ascontiguousarray(m4.T)
    onesc = np.ones((128, 1), dtype=ml_dtypes.bfloat16)
    onesr = np.ones((1, 128), dtype=ml_dtypes.bfloat16)
    shared = dict(wqkv=wqkv, wproj=wproj, bproj=bproj, gup16=gup16, dwn16=dwn16,
                  gsrep=gsrep, dsrep=dsrep, eye=eye, maskbd=maskbd, mask4=m4,
                  mask4t=m4t, onesc=onesc, onesr=onesr)
    in_maps = []
    for c in range(8):
        m = dict(shared)
        m["x"] = np.ascontiguousarray(x[c * T:(c + 1) * T])
        in_maps.append(m)
    return in_maps


def kernel(**inputs):
    in_maps = _host_prep(inputs)
    nc = build()
    res = run_bass_kernel_spmd(nc, in_maps, core_ids=list(range(8)))
    out = np.concatenate([np.asarray(res.results[c]["out"]) for c in range(8)], axis=0)
    return out.reshape(4, 4096, D)


# version 12
# speedup vs baseline: 1.1466x; 1.1466x over previous
import math
import os
import numpy as np
import ml_dtypes
from contextlib import ExitStack

import concourse.bass as bass
import concourse.mybir as mybir
from concourse.bass_utils import run_bass_kernel_spmd

F32 = mybir.dt.float32
BF16 = mybir.dt.bfloat16
FP16 = mybir.dt.float16
U16 = mybir.dt.uint16
AF = mybir.ActivationFunctionType
OP = mybir.AluOpType
AX = mybir.AxisListType

T = 2048
D = 2048
I = 5632
GU = 2 * I
NKC = 16
NTT = 16
LN_EPS = 1e-5
RMS_EPS = 1e-6
ATT_EPS = 1e-15
FTC = 512
NFT = T // FTC
QSP = 256
NQS = D // QSP


class Sem:
    def __init__(self, nc, ctx, name):
        self.h = ctx.enter_context(nc.semaphore(name))
        self.v = 0

    def inc(self, inst, n=1):
        inst.then_inc(self.h, n)
        self.v += n
        return self.v


def build():
    STAGE = int(os.environ.get("KSTAGE", "99"))
    DQOFF = os.environ.get("KDQOFF", "") == "1"
    QLIM = int(os.environ.get("KQLIM", "8"))
    KVLIM = int(os.environ.get("KKVLIM", "8"))
    TTLIM = int(os.environ.get("KTTLIM", str(NTT)))
    QMODE = os.environ.get("KQMODE", "full")
    YDUMP = os.environ.get("KYDUMP", "") == "1"
    WDBG = os.environ.get("KWDBG", "") == "1"
    nc = bass.Bass("TRN2", target_bir_lowering=False, debug=False,
                   detect_race_conditions=(os.environ.get("KRACE", "1") == "1"))
    dt = nc.dram_tensor
    x_h = dt("x", [T, D], F32, kind="ExternalInput").ap()
    wqkv_h = dt("wqkv", [3 * D, D], F32, kind="ExternalInput").ap()
    wproj_h = dt("wproj", [D, D], F32, kind="ExternalInput").ap()
    bproj_h = dt("bproj", [1, D], F32, kind="ExternalInput").ap()
    gup16_h = dt("gup16", [GU, 512], FP16, kind="ExternalInput").ap()
    dwn16_h = dt("dwn16", [D, 1408], FP16, kind="ExternalInput").ap()
    gsrep_h = dt("gsrep", [512, GU], FP16, kind="ExternalInput").ap()
    dsrep_h = dt("dsrep", [1408, D], FP16, kind="ExternalInput").ap()
    eye_h = dt("eye", [128, 128], BF16, kind="ExternalInput").ap()
    maskbd_h = dt("maskbd", [128, 128], BF16, kind="ExternalInput").ap()
    mask4_h = dt("mask4", [128, 4], BF16, kind="ExternalInput").ap()
    mask4t_h = dt("mask4t", [4, 128], BF16, kind="ExternalInput").ap()
    onesc_h = dt("onesc", [128, 1], BF16, kind="ExternalInput").ap()
    onesr_h = dt("onesr", [1, 128], BF16, kind="ExternalInput").ap()
    xbf = dt("xbf", [T, D], BF16, kind="Internal").ap()
    wqkvbf = dt("wqkvbf", [3 * D, D], BF16, kind="Internal").ap()
    wprojbf = dt("wprojbf", [D, D], BF16, kind="Internal").ap()
    x1f = dt("x1f", [T, D], F32, kind="ExternalOutput" if WDBG else "Internal").ap()
    x1bf = dt("x1bf", [T, D], BF16,
              kind="ExternalOutput" if WDBG else "Internal").ap()
    wkind = "ExternalOutput" if WDBG else "Internal"
    wgt = dt("wgt", [16, 128, I], BF16, kind=wkind).ap()
    wup = dt("wup", [16, 128, I], BF16, kind=wkind).ap()
    wdt = dt("wdt", [44, 128, D], BF16, kind=wkind).ap()
    cin = dt("cin", [128, 2064], F32, kind="Internal").ap()
    cfull = dt("cfull", [128, 2064], F32, kind="Internal").ap()
    out_h = dt("out", [T, D], F32, kind="ExternalOutput").ap()
    dbg_h = dt("dbg", [128, 8192], F32, kind="ExternalOutput").ap()

    ctx = ExitStack()
    with ctx:
        def ten(st, name, shape, dtp):
            return st.enter_context(nc.sbuf_tensor(name, shape, dtp))
        eye = ten(ctx, "eye_t", [128, 128], BF16)
        maskbd = ten(ctx, "maskbd_t", [128, 128], BF16)
        mask4 = ten(ctx, "mask4_t", [128, 4], BF16)
        mask4t = ten(ctx, "mask4t_t", [4, 128], BF16)
        onesc = ten(ctx, "onesc_t", [128, 1], BF16)
        onesr = ten(ctx, "onesr_t", [1, 128], BF16)
        cconst = ten(ctx, "cconst", [128, 4], U16)
        ecol = ten(ctx, "ecol", [128, 4], F32)
        stats = ten(ctx, "stats", [128, 16], F32)
        dq_u = [ten(ctx, f"dq_u{i}", [128, 512], FP16) for i in range(2)]
        dq_s = [ten(ctx, f"dq_s{i}", [128, 512], FP16) for i in range(2)]
        dq_m = ten(ctx, "dq_m", [128, 512], U16)
        dq_t2 = ten(ctx, "dq_t2", [128, 512], U16)
        dq_sb = ten(ctx, "dq_sb", [128, 512], U16)
        dq_o = [ten(ctx, f"dq_o{i}", [128, 2048], BF16) for i in range(2)]
        vkf = ten(ctx, "vkf", [128, 2064], F32)
        pb = [ctx.enter_context(nc.psum_tensor(f"pb{i}", [128, 512], F32)) for i in range(6)]
        pT = ctx.enter_context(nc.psum_tensor("pT", [128, 128], BF16))
        pks = ctx.enter_context(nc.psum_tensor("pks", [128, 16], F32))

        s_ld = Sem(nc, ctx, "s_ld")
        s_cast = Sem(nc, ctx, "s_cast")
        s_dqst = Sem(nc, ctx, "s_dqst")
        s_dqv = Sem(nc, ctx, "s_dqv")
        s_mm = Sem(nc, ctx, "s_mm")
        s_ev = Sem(nc, ctx, "s_ev")
        s_vv = Sem(nc, ctx, "s_vv")
        s_st = Sem(nc, ctx, "s_st")
        s_cc = Sem(nc, ctx, "s_cc")
        s_out = Sem(nc, ctx, "s_out")

        s1a = ExitStack()
        q_fm = ten(s1a, "q_fm", [128, NKC * T], BF16)
        bias_bc = ten(s1a, "bias_bc", [128, D], F32)
        bprow = vkf[32:33, 0:D]
        bprow_b = vkf[0:1, :].bitcast(BF16)[:, 0:D]
        s1b = ExitStack()
        x_fm = ten(s1b, "x_fm", [128, NKC * T], BF16)
        wsl = [ten(s1b, f"wsl{i}", [128, NKC * QSP], BF16) for i in range(2)]
        wslB = [ten(s1b, f"wslb{i}", [128, NKC * QSP], BF16) for i in range(2)]
        kslab = ten(s1b, "kslab", [128, 2 * QSP], BF16)
        vtile = [ten(s1b, f"vtile{i}", [128, QSP], BF16) for i in range(2)]
        vkx = ten(s1b, "vkx", [128, 2064], F32)

        block = ctx.enter_context(nc.Block())

        def finish(sem, val, scopes, dumps=()):
            @block.sync
            def _(sync):
                sync.wait_ge(sem.h, val)
                off = 0
                for ap in dumps:
                    w = ap.shape[-1]
                    sync.dma_start(out=dbg_h[0:ap.shape[0], off:off + w], in_=ap).then_inc(s_out.h, 16)
                    s_out.v += 16
                    off += w
                sync.wait_ge(s_out.h, s_out.v)

            @block.gpsimd
            def _(g):
                g.wait_ge(s_dqst.h, s_dqst.v)
                g.wait_ge(s_cast.h, s_cast.v)
            for sc in scopes:
                sc.close()

        @block.sync
        def _(sync):
            for dst_t, src in [(eye[:, :], eye_h), (maskbd[:, :], maskbd_h), (mask4[:, :], mask4_h),
                               (mask4t[:, :], mask4t_h), (onesc[:, :], onesc_h), (onesr[:, :], onesr_h),
                               (bprow, bproj_h)]:
                s_ld.inc(sync.dma_start(out=dst_t, in_=src[:, :]), 16)
        P0_CONST = s_ld.v

        @block.gpsimd
        def _(g):
            for i in range(4):
                s_cast.inc(g.dma_start(out=xbf[i * 512:(i + 1) * 512, :],
                                       in_=x_h[i * 512:(i + 1) * 512, :]), 16)
        CAST_X = s_cast.v

        @block.vector
        def _(v):
            v.memset(cconst[:, 0:1], 252)
            v.memset(cconst[:, 1:2], 6)
            v.memset(ecol[:, 0:1], LN_EPS)
            v.memset(ecol[:, 1:2], RMS_EPS)
            v.memset(ecol[:, 2:3], ATT_EPS)
            s_vv.inc(v.engine_nop())

        @block.scalar
        def _(s):
            s.wait_ge(s_ld.h, P0_CONST)
            s_ev.inc(s.activation(out=bprow_b, in_=bprow, func=AF.Copy))
        BPROWB = s_ev.v

        @block.tensor
        def _(t):
            t.wait_ge(s_ev.h, BPROWB)
            for sp in range(4):
                s_mm.inc(t.matmul(pb[sp][:, :], onesr[:, :], bprow_b[0:1, sp * 512:(sp + 1) * 512],
                                  start=True, stop=True))
        BIAS_MM = s_mm.v

        @block.vector
        def _(v):
            v.wait_ge(s_mm.h, BIAS_MM)
            for sp in range(4):
                v.tensor_copy(bias_bc[:, sp * 512:(sp + 1) * 512], pb[sp][:, :])
            s_vv.inc(v.engine_nop())
        BIAS_BC = s_vv.v

        @block.gpsimd
        def _(g):
            for i in range(6):
                s_cast.inc(g.dma_start(out=wqkvbf[i * 1024:(i + 1) * 1024, :],
                                       in_=wqkv_h[i * 1024:(i + 1) * 1024, :]), 16)
            for i in range(2):
                s_cast.inc(g.dma_start(out=wprojbf[i * 1024:(i + 1) * 1024, :],
                                       in_=wproj_h[i * 1024:(i + 1) * 1024, :]), 16)
        CAST_QKV = [CAST_X + 16 * (i + 1) for i in range(6)]
        CAST_PROJ = s_cast.v

        @block.sync
        def _(sync):
            sync.wait_ge(s_cast.h, CAST_X)
            for kc in range(NKC):
                s_ld.inc(sync.dma_start(out=x_fm[:, kc * T:(kc + 1) * T],
                                        in_=xbf[:, kc * 128:(kc + 1) * 128],
                                        transpose=True), 16)
        XFM_DONE = s_ld.v
        if STAGE <= 1:
            finish(s_ld, XFM_DONE, [s1b, s1a],
                   dumps=[x_fm[:, 0:T].bitcast(F32), vkf[:, :]])
            return nc

        dq_work = []
        for p0 in range(11):
            for half in range(2):
                for wc in range(4):
                    dq_work.append(("gu", wc, half, p0, 1))
        for wc in range(11):
            for p0 in range(4):
                dq_work.append(("d", wc, 0, p0, 1))
        dq_items = []
        for piece in dq_work:
            dq_items.append(("slab",) + piece)
            for s_i in range(piece[4]):
                dq_items.append(("grp",) + piece + (s_i,))
        DQ_GATE_STORES = 88 * 16
        DQ_ALL_STORES = 132 * 16
        DQ_SPAN_THRESH = {}
        _cnt = 0
        for it in dq_items:
            if it[0] == "grp":
                _cnt += 16
                _, _kind, _wc, _half, _p0, _nsp, _si = it
                if _kind == "gu" and _half == 1 and _wc == 3 and _si == _nsp - 1:
                    for _spn in range(2 * _p0, 2 * (_p0 + _nsp)):
                        DQ_SPAN_THRESH[_spn] = _cnt
        dq_pos = [0]
        dq_ring = [0]
        dq_ngrp = [0]
        dq_slab_tgt = [0]
        dq_su = [0]
        dq_sguard = [0, 0]
        dq_cur = [0]

        def pump_dq(n):
            if DQOFF:
                return
            done = 0
            while done < n and dq_pos[0] < len(dq_items):
                item = dq_items[dq_pos[0]]
                if item[0] == "slab":
                    _, kind, wc, half, p0, nsp = item
                    ur = dq_su[0] & 1
                    dq_su[0] += 1
                    dq_cur[0] = ur
                    guard = dq_sguard[ur]

                    @block.sync
                    def _(sync, kind=kind, wc=wc, half=half, p0=p0, nsp=nsp,
                          guard=guard, ur=ur):
                        if guard:
                            sync.wait_ge(s_dqv.h, guard)
                        w = nsp * 512
                        if kind == "gu":
                            r0 = half * I + p0 * 512
                            s_ld.inc(sync.dma_start(out=dq_u[ur][:, 0:w],
                                                    in_=gup16_h[r0:r0 + w, wc * 128:(wc + 1) * 128],
                                                    transpose=True), 16)
                            s_ld.inc(sync.dma_start(out=dq_s[ur][:, 0:w],
                                                    in_=gsrep_h[wc * 128:(wc + 1) * 128, r0:r0 + w]), 16)
                        else:
                            r0 = p0 * 512
                            s_ld.inc(sync.dma_start(out=dq_u[ur][:, 0:w],
                                                    in_=dwn16_h[r0:r0 + w, wc * 128:(wc + 1) * 128],
                                                    transpose=True), 16)
                            s_ld.inc(sync.dma_start(out=dq_s[ur][:, 0:w],
                                                    in_=dsrep_h[wc * 128:(wc + 1) * 128, r0:r0 + w]), 16)
                    dq_slab_tgt[0] = s_ld.v
                    dq_pos[0] += 1
                else:
                    _, kind, wc, half, p0, nsp, s_i = item
                    ring = dq_ring[0]
                    dq_ring[0] ^= 1
                    tgt_ld = dq_slab_tgt[0]
                    ngrp = dq_ngrp[0]

                    ur = dq_cur[0]

                    @block.vector
                    def _(v, kind=kind, wc=wc, half=half, p0=p0, s_i=s_i,
                          ring=ring, tgt_ld=tgt_ld, ngrp=ngrp, ur=ur):
                        ot = dq_o[ring]
                        v.wait_ge(s_ld.h, tgt_ld)
                        if ngrp >= 2:
                            v.wait_ge(s_dqst.h, (ngrp - 1) * 16)
                        off = s_i * 512
                        u = dq_u[ur][:, off:off + 512].bitcast(U16)
                        for j in range(4):
                            if j == 0:
                                v.tensor_scalar(dq_m[:, :], u, 7, None, OP.bitwise_and)
                            else:
                                v.tensor_scalar(dq_m[:, :], u, 4 * j, 7,
                                                OP.logical_shift_right, OP.bitwise_and)
                            v.tensor_scalar(dq_t2[:, :], dq_m[:, :], 252, None, OP.mult)
                            v.scalar_tensor_tensor(dq_t2[:, :], dq_m[:, :], cconst[:, 0:1],
                                                   dq_t2[:, :], OP.add, OP.min)
                            if j == 3:
                                v.tensor_scalar(dq_sb[:, :], u, 32768, None, OP.bitwise_and)
                            else:
                                v.tensor_scalar(dq_sb[:, :], u, 12 - 4 * j, 32768,
                                                OP.logical_shift_left, OP.bitwise_and)
                            v.scalar_tensor_tensor(dq_sb[:, :], dq_t2[:, :], cconst[:, 1:2],
                                                   dq_sb[:, :], OP.logical_shift_left, OP.bitwise_or)
                            if kind == "d":
                                v.tensor_tensor(ot[:, j * 512:(j + 1) * 512],
                                                dq_sb[:, :].bitcast(BF16),
                                                dq_s[ur][:, off:off + 512], OP.mult)
                            else:
                                oap = ot[:, j * 512:(j + 1) * 512].rearrange("p (b a) -> p b a", b=4)
                                iap = dq_sb[:, :].bitcast(BF16).rearrange("p (a b) -> p b a", a=128)
                                sap = dq_s[ur][:, off:off + 512].rearrange("p (a b) -> p b a", a=128)
                                v.tensor_tensor(oap, iap, sap, OP.mult)
                        s_dqv.inc(v.engine_nop())
                    tgt_v = s_dqv.v
                    dq_sguard[ur] = tgt_v

                    @block.gpsimd
                    def _(g, kind=kind, wc=wc, half=half, p0=p0, s_i=s_i, ring=ring, tgt_v=tgt_v):
                        g.wait_ge(s_dqv.h, tgt_v)
                        ot = dq_o[ring]
                        osg = p0 + s_i
                        if kind == "d":
                            ic0 = wc * 4
                            dst = wdt[ic0:ic0 + 4, :, osg * 512:(osg + 1) * 512]
                        else:
                            tgt_t = wgt if half == 0 else wup
                            dst = tgt_t[wc:16:4, :, osg * 512:(osg + 1) * 512]
                        s_dqst.inc(g.dma_start(out=dst.transpose([1, 0, 2]),
                                               in_=ot[:, :].rearrange("p (j f) -> p j f", j=4)), 16)
                    dq_ngrp[0] += 1
                    dq_pos[0] += 1
                    done += 1

        Q_MM0 = s_mm.v
        Q_EV0 = s_ev.v
        for osp in range(min(NQS, QLIM)):
            buf = wsl[osp % 2]

            @block.sync
            def _(sync, osp=osp, buf=buf):
                if osp == 0:
                    sync.wait_ge(s_ld.h, XFM_DONE)
                if osp >= 2:
                    sync.wait_ge(s_mm.h, Q_MM0 + (osp - 1) * 8)
                sync.wait_ge(s_cast.h, CAST_QKV[min(5, ((osp + 1) * QSP - 1) // 1024)])
                for kc in range(NKC):
                    s_ld.inc(sync.dma_start(out=buf[:, kc * QSP:(kc + 1) * QSP],
                                            in_=wqkvbf[osp * QSP:(osp + 1) * QSP,
                                                       kc * 128:(kc + 1) * 128],
                                            transpose=True), 16)
            Q_LD = s_ld.v

            if QMODE != "ldonly":
                @block.tensor
                def _(t, osp=osp, buf=buf, Q_LD=Q_LD):
                    t.wait_ge(s_ld.h, Q_LD)
                    if osp == 0:
                        t.wait_ge(s_vv.h, BIAS_BC)
                    for och in range(2):
                        for tsp in range(4):
                            gidx = osp * 8 + och * 4 + tsp
                            bank = pb[gidx % 2]
                            if gidx >= 2 and QMODE == "full":
                                t.wait_ge(s_ev.h, Q_EV0 + gidx - 1)
                            for kc in range(NKC):
                                mi = t.matmul(bank[:, :],
                                              buf[:, kc * QSP + och * 128: kc * QSP + (och + 1) * 128],
                                              x_fm[:, kc * T + tsp * 512: kc * T + (tsp + 1) * 512],
                                              start=(kc == 0), stop=(kc == NKC - 1))
                            s_mm.inc(mi)

            if QMODE == "full":
                @block.scalar
                def _(s, osp=osp):
                    for och in range(2):
                        for tsp in range(4):
                            gidx = osp * 8 + och * 4 + tsp
                            bank = pb[gidx % 2]
                            s.wait_ge(s_mm.h, Q_MM0 + gidx + 1)
                            oc = osp * 2 + och
                            s_ev.inc(s.activation(
                                out=q_fm[:, oc * T + tsp * 512: oc * T + (tsp + 1) * 512],
                                in_=bank[:, :], func=AF.Relu))
            pump_dq(10)
        Q_PE_DONE = Q_MM0 + min(NQS, QLIM) * 8
        if STAGE == 12:
            @block.scalar
            def _(s):
                pass
            finish(s_ev, s_ev.v, [s1b, s1a])
            return nc

        kv_mm = {}
        kv_ev = {}
        KV_MM0 = s_mm.v
        for sp in range(min(NQS, KVLIM)):
            bufk = wsl[0] if sp % 2 == 0 else wslB[0]
            bufv = wsl[1] if sp % 2 == 0 else wslB[1]
            vk_bank = pb[4 + (sp % 2)]

            @block.sync
            def _(sync, sp=sp, bufk=bufk, bufv=bufv):
                if sp == 0:
                    sync.wait_ge(s_cast.h, CAST_QKV[5])
                    sync.wait_ge(s_mm.h, Q_PE_DONE)
                elif sp == 1:
                    sync.wait_ge(s_cast.h, CAST_QKV[5])
                else:
                    sync.wait_ge(s_mm.h, kv_mm[(sp - 2, "pe_done")])
                for kc in range(NKC):
                    s_ld.inc(sync.dma_start(out=bufk[:, kc * QSP:(kc + 1) * QSP],
                                            in_=wqkvbf[2048 + sp * QSP: 2048 + (sp + 1) * QSP,
                                                       kc * 128:(kc + 1) * 128],
                                            transpose=True), 16)
                for kc in range(NKC):
                    s_ld.inc(sync.dma_start(out=bufv[:, kc * QSP:(kc + 1) * QSP],
                                            in_=wqkvbf[4096 + sp * QSP: 4096 + (sp + 1) * QSP,
                                                       kc * 128:(kc + 1) * 128],
                                            transpose=True), 16)
            kv_mm[(sp, "wld")] = s_ld.v
            if STAGE == 17 and sp == 0:
                finish(s_ld, kv_mm[(0, "wld")], [s1b, s1a],
                       dumps=[bufk[:, :].bitcast(F32), bufv[:, :].bitcast(F32)])
                return nc
            ev0 = s_ev.v
            kv_ev[sp] = ev0
            mm0 = s_mm.v
            kv_mm[(sp, "mm0")] = mm0

            @block.tensor
            def _(t, sp=sp, bufk=bufk, bufv=bufv, vk_bank=vk_bank, ev0=ev0, mm0=mm0):
                t.wait_ge(s_ld.h, kv_mm[(sp, "wld")])
                if sp >= 1:
                    t.wait_ge(s_ev.h, kv_mm[(sp - 1, "ev_done")])
                if sp >= 2:
                    t.wait_ge(s_vv.h, kv_mm[(sp - 2, "vkev")])
                for tt in range(TTLIM):
                    if tt >= 1:
                        t.wait_ge(s_ev.h, ev0 + 2 * (tt - 1) + 1)
                    for kc in range(NKC):
                        mi = t.matmul(pb[2][:, 0:QSP],
                                      x_fm[:, kc * T + tt * 128: kc * T + (tt + 1) * 128],
                                      bufk[:, kc * QSP:(kc + 1) * QSP],
                                      start=(kc == 0), stop=(kc == NKC - 1))
                    s_mm.inc(mi)
                    if tt >= 1:
                        t.wait_ge(s_ev.h, ev0 + 2 * (tt - 1) + 2)
                    for kc in range(NKC):
                        mi = t.matmul(pb[3][:, 0:QSP],
                                      x_fm[:, kc * T + tt * 128: kc * T + (tt + 1) * 128],
                                      bufv[:, kc * QSP:(kc + 1) * QSP],
                                      start=(kc == 0), stop=(kc == NKC - 1))
                    s_mm.inc(mi)
                    if os.environ.get("KNOVK", "") == "1":
                        s_mm.inc(mi)
                        continue
                    t.wait_ge(s_ev.h, ev0 + 2 * tt + 2)
                    vb = vtile[tt % 2]
                    for cl in range(2):
                        t.matmul(vk_bank[:, cl * 128:(cl + 1) * 128],
                                 vb[:, cl * 128:(cl + 1) * 128],
                                 kslab[:, (tt % 2) * QSP + cl * 128: (tt % 2) * QSP + (cl + 1) * 128],
                                 start=(tt == 0 and cl == 0), stop=(tt == TTLIM - 1 and cl == 1))
                        mi = t.matmul(pks[:, (sp * 2 + cl):(sp * 2 + cl) + 1],
                                      kslab[:, (tt % 2) * QSP + cl * 128: (tt % 2) * QSP + (cl + 1) * 128],
                                      onesc[:, :],
                                      start=(sp == 0 and tt == 0 and cl == 0),
                                      stop=(sp == min(NQS, KVLIM) - 1 and tt == TTLIM - 1 and cl == 1))
                    s_mm.inc(mi)
            kv_mm[(sp, "pe_done")] = s_mm.v

            @block.scalar
            def _(s, sp=sp, mm0=mm0):
                if os.environ.get("KNOKEV", "") == "1":
                    return
                for tt in range(TTLIM):
                    s.wait_ge(s_mm.h, mm0 + 3 * tt + 1)
                    if tt >= 2:
                        s.wait_ge(s_mm.h, mm0 + 3 * (tt - 2) + 3)
                    s_ev.inc(s.activation(out=kslab[:, (tt % 2) * QSP:(tt % 2 + 1) * QSP],
                                          in_=pb[2][:, 0:QSP], func=AF.Relu))
                    s.wait_ge(s_mm.h, mm0 + 3 * tt + 2)
                    s_ev.inc(s.activation(out=vtile[tt % 2][:, :],
                                          in_=pb[3][:, 0:QSP], func=AF.Copy))
            kv_mm[(sp, "ev_done")] = s_ev.v

            @block.vector
            def _(v, sp=sp, vk_bank=vk_bank):
                v.wait_ge(s_mm.h, kv_mm[(sp, "pe_done")])
                v.tensor_copy(vkx[:, sp * QSP:(sp + 1) * QSP], vk_bank[:, 0:QSP])
                s_vv.inc(v.engine_nop())
            kv_mm[(sp, "vkev")] = s_vv.v
            pump_dq(13)

        @block.vector
        def _(v):
            v.wait_ge(s_mm.h, kv_mm[(min(NQS, KVLIM) - 1, "pe_done")])
            v.tensor_copy(vkx[:, 2048:2064], pks[:, :])
            s_vv.inc(v.engine_nop())
        VKX_DONE = s_vv.v
        if STAGE <= 2:
            finish(s_vv, VKX_DONE, [s1b, s1a],
                   dumps=() if os.environ.get("KNODMP", "") == "1" else (
                          vkx[:, :],
                          q_fm[:, 0:T].bitcast(F32),
                          x_fm[:, 0:T].bitcast(F32),
                          kslab[:, :].bitcast(F32)))
            return nc

        @block.scalar
        def _(s):
            s.wait_ge(s_vv.h, VKX_DONE)
            s_st.inc(s.dma_start(out=cin[:, :], in_=vkx[:, :]), 16)
        CIN_ST = s_st.v

        @block.gpsimd
        def _(g):
            g.wait_ge(s_st.h, CIN_ST)
            s_cc.inc(g.collective_compute(
                "AllReduce", OP.add,
                replica_groups=[[0, 1], [2, 3], [4, 5], [6, 7]],
                ins=[cin[:, :].opt()],
                outs=[cfull[:, :].opt()]), 1)

        @block.sync
        def _(sync):
            sync.wait_ge(s_cc.h, s_cc.v)
            s_ld.inc(sync.dma_start(out=vkf[:, :], in_=cfull[:, :]), 16)
        VKF_LD = s_ld.v
        if STAGE <= 3:
            finish(s_ld, VKF_LD, [s1b, s1a])
            return nc

        pump_dq(8)
        s1b.close()
        s2 = ExitStack()
        wpT = ten(s2, "wpT", [128, NKC * 2048], BF16)
        rden = ten(s2, "rden", [128, 2048], BF16)
        xres = [ten(s2, f"xres{i}", [128, D], F32) for i in range(2)]
        rbuf = ten(s2, "rbuf", [128, D], F32)
        x1b = [ten(s2, "x1b0", [128, D], BF16)] * 2
        tmp128 = ten(s2, "tmp128", [128, 128], BF16)
        Abuf = [ten(s2, f"Abuf{i}", [128, 128], BF16) for i in range(2)]
        cbuf = ten(s2, "cbuf", [128, 4], BF16)
        den4b = ten(s2, "den4b", [4, 2048], BF16)

        @block.sync
        def _(sync):
            sync.wait_ge(s_cast.h, CAST_PROJ)
            sync.wait_ge(s_mm.h, kv_mm[(NQS - 1, "pe_done")])
            sync.wait_ge(s_st.h, CIN_ST)
            for kc in range(NKC):
                for osp in range(4):
                    s_ld.inc(sync.dma_start(
                        out=wpT[:, kc * 2048 + osp * 512: kc * 2048 + (osp + 1) * 512],
                        in_=wprojbf[osp * 512:(osp + 1) * 512, kc * 128:(kc + 1) * 128],
                        transpose=True), 16)
        WPT_DONE = s_ld.v

        ph4 = {}
        for c in range(NKC):
            @block.vector
            def _(v, c=c):
                if c == 0:
                    v.wait_ge(s_ld.h, VKF_LD)
                if c >= 1:
                    v.wait_ge(s_mm.h, ph4[(c - 1, "denmm", 3)])
                v.tensor_tensor(tmp128[:, :], vkf[:, c * 128:(c + 1) * 128], maskbd[:, :], OP.mult)
                v.tensor_scalar(cbuf[:, :], mask4[:, :], vkf[:, 2048 + c:2049 + c], None, OP.mult)
                s_vv.inc(v.engine_nop())
            ph4[(c, "prep")] = s_vv.v

            @block.tensor
            def _(t, c=c):
                t.wait_ge(s_vv.h, ph4[(c, "prep")])
                if c >= 1:
                    t.wait_ge(s_ev.h, ph4[(c - 1, "Aev")])
                s_mm.inc(t.transpose(pT[:, :], tmp128[:, :], eye[:, :]))
            ph4[(c, "tr")] = s_mm.v

            @block.scalar
            def _(s, c=c):
                s.wait_ge(s_mm.h, ph4[(c, "tr")])
                s_ev.inc(s.activation(out=Abuf[c % 2][:, :], in_=pT[:, :], func=AF.Copy))
            ph4[(c, "Aev")] = s_ev.v

            den_mm = []
            den_ev = []
            dbc_mm = []
            dbc_ev = []

            DEN_PRED = s_vv.v

            @block.tensor
            def _(t, c=c, DEN_PRED=DEN_PRED):
                t.wait_ge(s_ev.h, ph4[(c, "Aev")])
                for tsp in range(4):
                    bank = pb[2 + tsp % 2]
                    if tsp >= 2:
                        t.wait_ge(s_vv.h, DEN_PRED + tsp - 1)
                    elif c >= 1:
                        t.wait_ge(s_ev.h, ph4[(c - 1, "dbcev", 2 + tsp)])
                    s_mm.inc(t.matmul(bank[0:4, :], cbuf[:, :],
                             q_fm[:, c * T + tsp * 512: c * T + (tsp + 1) * 512],
                             start=True, stop=True))
                    ph4[(c, "denmm", tsp)] = s_mm.v

            @block.vector
            def _(v, c=c):
                for tsp in range(4):
                    bank = pb[2 + tsp % 2]
                    v.wait_ge(s_mm.h, ph4[(c, "denmm", tsp)])
                    v.tensor_scalar(den4b[:, tsp * 512:(tsp + 1) * 512], bank[0:4, :],
                                    ecol[0:4, 2:3], None, OP.add)
                    s_vv.inc(v.engine_nop())
                    ph4[(c, "denev", tsp)] = s_vv.v

            OUTEV_PRED = s_vv.v

            @block.tensor
            def _(t, c=c, OUTEV_PRED=OUTEV_PRED):
                for tsp in range(4):
                    bank = pb[2 + tsp % 2]
                    t.wait_ge(s_vv.h, ph4[(c, "denev", tsp + 2 if tsp < 2 else 3)])
                    if tsp >= 2:
                        t.wait_ge(s_ev.h, ph4[(c, "Aev")] + 2 * (tsp - 1))
                    s_mm.inc(t.matmul(bank[:, :], mask4t[:, :],
                             den4b[:, tsp * 512:(tsp + 1) * 512], start=True, stop=True))
                    ph4[(c, "dbcmm", tsp)] = s_mm.v
                for tsp in range(4):
                    bank = pb[tsp % 2]
                    if tsp >= 2:
                        t.wait_ge(s_vv.h, OUTEV_PRED + tsp - 1)
                    elif c >= 1:
                        t.wait_ge(s_vv.h, ph4[(c - 1, "outev", tsp + 2)])
                    s_mm.inc(t.matmul(bank[:, :], Abuf[c % 2][:, :],
                             q_fm[:, c * T + tsp * 512: c * T + (tsp + 1) * 512],
                             start=True, stop=True))
                    ph4[(c, "outmm", tsp)] = s_mm.v

            @block.scalar
            def _(s, c=c):
                for tsp in range(4):
                    bank = pb[2 + tsp % 2]
                    s.wait_ge(s_mm.h, ph4[(c, "dbcmm", tsp)])
                    s_ev.inc(s.activation(out=rden[:, tsp * 512:(tsp + 1) * 512],
                                          in_=bank[:, :], func=AF.Ln))
                    s_ev.inc(s.activation(out=rden[:, tsp * 512:(tsp + 1) * 512],
                                          in_=rden[:, tsp * 512:(tsp + 1) * 512],
                                          func=AF.Exp, scale=-1.0))
                    ph4[(c, "dbcev", tsp)] = s_ev.v

            @block.vector
            def _(v, c=c):
                for tsp in range(4):
                    bank = pb[tsp % 2]
                    v.wait_ge(s_mm.h, ph4[(c, "outmm", tsp)])
                    v.wait_ge(s_ev.h, ph4[(c, "dbcev", tsp)])
                    v.tensor_tensor(q_fm[:, c * T + tsp * 512: c * T + (tsp + 1) * 512],
                                    bank[:, :], rden[:, tsp * 512:(tsp + 1) * 512], OP.mult)
                    s_vv.inc(v.engine_nop())
                    ph4[(c, "outev", tsp)] = s_vv.v
            if c % 2 == 1:
                pump_dq(1)
        P4_DONE_VV = s_vv.v
        if STAGE <= 4:
            finish(s_vv, P4_DONE_VV, [s2, s1a],
                   dumps=[q_fm[:, 0:T].bitcast(F32), rden[:, :].bitcast(F32), vkf[:, :]])
            return nc

        ph5 = {}
        for tt in range(NTT):
            @block.sync
            def _(sync, tt=tt):
                if tt == 0:
                    sync.wait_ge(s_ld.h, WPT_DONE)
                    sync.wait_ge(s_vv.h, P4_DONE_VV)
                if tt >= 2:
                    sync.wait_ge(s_vv.h, ph5[(tt - 2, "rsum")])
                s_ld.inc(sync.dma_start(out=xres[tt % 2][:, :],
                                        in_=x_h[tt * 128:(tt + 1) * 128, :]), 16)
            ph5[(tt, "xld")] = s_ld.v

            @block.vector
            def _(v, tt=tt):
                v.wait_ge(s_ld.h, ph5[(tt, "xld")])
                v.tensor_tensor(xres[tt % 2][:, :], xres[tt % 2][:, :], bias_bc[:, :], OP.add)
                s_vv.inc(v.engine_nop())
                ph5[(tt, "xb")] = s_vv.v

            @block.tensor
            def _(t, tt=tt):
                if tt == 0:
                    t.wait_ge(s_vv.h, P4_DONE_VV)
                    t.wait_ge(s_ev.h, ph4[(NKC - 1, "dbcev", 3)])
                for osp in range(4):
                    gidx = tt * 4 + osp
                    bank = pb[gidx % 4]
                    if gidx >= 4:
                        t.wait_ge(s_vv.h, ph5[(gidx - 4, "prjev")])
                    for kc in range(NKC):
                        mi = t.matmul(bank[:, :],
                                      q_fm[:, kc * T + tt * 128: kc * T + (tt + 1) * 128],
                                      wpT[:, kc * 2048 + osp * 512: kc * 2048 + (osp + 1) * 512],
                                      start=(kc == 0), stop=(kc == NKC - 1))
                    s_mm.inc(mi)
                    ph5[(gidx, "prjmm")] = s_mm.v

            @block.vector
            def _(v, tt=tt):
                v.wait_ge(s_vv.h, ph5[(tt, "xb")])
                if tt >= 1:
                    v.wait_ge(s_st.h, ph5[(tt - 1, "x1st")])
                    v.wait_ge(s_ev.h, ph5[(tt - 1, "x1b")])
                for osp in range(4):
                    gidx = tt * 4 + osp
                    bank = pb[gidx % 4]
                    v.wait_ge(s_mm.h, ph5[(gidx, "prjmm")])
                    v.scalar_tensor_tensor(rbuf[:, osp * 512:(osp + 1) * 512],
                                           bank[:, :], 1.0,
                                           xres[tt % 2][:, osp * 512:(osp + 1) * 512],
                                           OP.bypass, OP.add)
                    s_vv.inc(v.engine_nop())
                    ph5[(gidx, "prjev")] = s_vv.v
                ph5[(tt, "rsum")] = s_vv.v

            @block.scalar
            def _(s, tt=tt):
                s.wait_ge(s_vv.h, ph5[(tt, "rsum")])
                s.activation(out=rden[:, :], in_=rbuf[:, :], func=AF.Identity,
                             scale=1.0 / D, accum_out=stats[:, 2:3])
                s.activation(out=rden[:, :], in_=rbuf[:, :], func=AF.Square,
                             scale=1.0 / math.sqrt(D), accum_out=stats[:, 3:4])
                s_ev.inc(s.drain())
                ph5[(tt, "sq")] = s_ev.v

            @block.vector
            def _(v, tt=tt):
                v.wait_ge(s_ev.h, ph5[(tt, "sq")])
                v.scalar_tensor_tensor(stats[:, 5:6], stats[:, 2:3], stats[:, 2:3],
                                       stats[:, 3:4], OP.mult, OP.subtract)
                s_vv.inc(v.engine_nop())
                ph5[(tt, "var")] = s_vv.v

            @block.scalar
            def _(s, tt=tt):
                s.wait_ge(s_vv.h, ph5[(tt, "var")])
                s.activation(out=stats[:, 6:7], in_=stats[:, 5:6], func=AF.Ln,
                             bias=ecol[:, 0:1], scale=-1.0)
                s_ev.inc(s.activation(out=stats[:, 7:8], in_=stats[:, 6:7], func=AF.Exp,
                                      scale=-0.5))
                ph5[(tt, "rstd")] = s_ev.v

            @block.vector
            def _(v, tt=tt):
                v.wait_ge(s_ev.h, ph5[(tt, "rstd")])
                v.tensor_scalar(rbuf[:, :], rbuf[:, :], stats[:, 2:3], stats[:, 7:8],
                                OP.subtract, OP.mult)
                s_vv.inc(v.engine_nop())
                ph5[(tt, "x1")] = s_vv.v

            @block.scalar
            def _(s, tt=tt):
                s.wait_ge(s_vv.h, ph5[(tt, "x1")])
                if tt >= 1:
                    s.wait_ge(s_st.h, ph5[(tt - 1, "x1st")])
                for j in range(4):
                    ai = s.activation(out=x1b[tt % 2][:, j * 512:(j + 1) * 512],
                                      in_=rbuf[:, j:D:4], func=AF.Copy)
                s_ev.inc(ai)
                ph5[(tt, "x1b")] = s_ev.v
                s_st.inc(s.dma_start(out=x1f[tt * 128:(tt + 1) * 128, :], in_=rbuf[:, :]), 16)
                s.wait_ge(s_ev.h, ph5[(tt, "x1b")])
                s_st.inc(s.dma_start(out=x1bf[tt * 128:(tt + 1) * 128, :], in_=x1b[tt % 2][:, :]), 16)
            ph5[(tt, "x1st")] = s_st.v

            pump_dq(3)
        P5_PE_DONE = ph5[(NTT * 4 - 1, "prjmm")]
        P5_ST_DONE = ph5[(NTT - 1, "x1st")]
        P5_EV_DONE = ph5[(NTT - 1, "x1b")]
        P5_VV_DONE = s_vv.v
        if STAGE <= 5:
            finish(s_st, P5_ST_DONE, [s2, s1a],
                   dumps=[rbuf[:, :], stats[:, :]])
            return nc

        s2.close()
        s1a.close()
        s3 = ExitStack()
        x1fm = ten(s3, "x1fm", [128, NKC * FTC], BF16)
        wgu = [ten(s3, f"wgu{i}", [128, NKC * 256], BF16) for i in range(2)]
        wub = [ten(s3, f"wub{i}", [128, NKC * 256], BF16) for i in range(2)]
        hbuf = ten(s3, "hbuf", [128, 44 * FTC], BF16)
        sgt = ten(s3, "sgt", [128, FTC], BF16)
        wdsl = [ten(s3, f"wdsl{i}", [128, 11 * 512], BF16) for i in range(2)]
        y2 = [ten(s3, f"y2_{i}", [128, D], F32) for i in range(2)]
        xr2 = [ten(s3, "xr2_0", [128, D], F32)] * 2
        junk3 = ten(s3, "junk3", [128, D], BF16)

        pump_dq(10 ** 9)

        ffn = {}
        S3_GUARD_EMITTED = [False]
        for tc in range(NFT):
            @block.sync
            def _(sync, tc=tc):
                if not S3_GUARD_EMITTED[0]:
                    sync.wait_ge(s_mm.h, P5_PE_DONE)
                    sync.wait_ge(s_st.h, P5_ST_DONE)
                    sync.wait_ge(s_ev.h, P5_EV_DONE)
                    sync.wait_ge(s_vv.h, P5_VV_DONE)
                    S3_GUARD_EMITTED[0] = True
                sync.wait_ge(s_st.h, ph5[(tc * 4 + 3, "x1st")])
                if tc >= 1:
                    sync.wait_ge(s_mm.h, ffn[(tc - 1, "gu_pe")])
                for kc in range(NKC):
                    s_ld.inc(sync.dma_start(out=x1fm[:, kc * FTC:(kc + 1) * FTC],
                                            in_=x1bf[tc * FTC:(tc + 1) * FTC,
                                                     kc * 128:(kc + 1) * 128],
                                            transpose=True), 16)
            ffn[(tc, "x1fm")] = s_ld.v

            GSPN = 22
            for spn in range(GSPN):
                @block.sync
                def _(sync, tc=tc, spn=spn):
                    gi = tc * GSPN + spn
                    if tc == 0 and not DQOFF:
                        sync.wait_ge(s_dqst.h, DQ_SPAN_THRESH[spn])
                    if gi >= 2:
                        sync.wait_ge(s_mm.h, ffn[(gi - 2, "gu_sp_pe")])
                    ring = spn % 2
                    s_ld.inc(sync.dma_start(
                        out=wgu[ring][:, :].rearrange("p (k f) -> p k f", k=16),
                        in_=wgt[0:16, :, spn * 256:(spn + 1) * 256].transpose([1, 0, 2])), 16)
                    s_ld.inc(sync.dma_start(
                        out=wub[ring][:, :].rearrange("p (k f) -> p k f", k=16),
                        in_=wup[0:16, :, spn * 256:(spn + 1) * 256].transpose([1, 0, 2])), 16)
                ffn[(tc * GSPN + spn, "gu_ld")] = s_ld.v

                @block.tensor
                def _(t, tc=tc, spn=spn):
                    gi = tc * GSPN + spn
                    t.wait_ge(s_ld.h, ffn[(gi, "gu_ld")])
                    if spn == 0:
                        t.wait_ge(s_ld.h, ffn[(tc, "x1fm")])
                    ring = spn % 2
                    for ol in range(2):
                        ic = spn * 2 + ol
                        bg = pb[0 if ic % 2 == 0 else 2]
                        bu = pb[1 if ic % 2 == 0 else 3]
                        if ic >= 2:
                            t.wait_ge(s_vv.h, ffn[(tc, "h", ic - 2)])
                        for kc in range(NKC):
                            mi = t.matmul(bg[:, :],
                                          wgu[ring][:, kc * 256 + ol * 128: kc * 256 + (ol + 1) * 128],
                                          x1fm[:, kc * FTC:(kc + 1) * FTC],
                                          start=(kc == 0), stop=(kc == NKC - 1))
                        s_mm.inc(mi)
                        ffn[(tc, "gmm", ic)] = s_mm.v
                        for kc in range(NKC):
                            mi = t.matmul(bu[:, :],
                                          wub[ring][:, kc * 256 + ol * 128: kc * 256 + (ol + 1) * 128],
                                          x1fm[:, kc * FTC:(kc + 1) * FTC],
                                          start=(kc == 0), stop=(kc == NKC - 1))
                        s_mm.inc(mi)
                        ffn[(tc, "umm", ic)] = s_mm.v
                ffn[(tc * GSPN + spn, "gu_sp_pe")] = s_mm.v

                H_PRED = s_vv.v

                @block.scalar
                def _(s, tc=tc, spn=spn, H_PRED=H_PRED):
                    for ol in range(2):
                        ic = spn * 2 + ol
                        bg = pb[0 if ic % 2 == 0 else 2]
                        s.wait_ge(s_mm.h, ffn[(tc, "gmm", ic)])
                        if ol == 1:
                            s.wait_ge(s_vv.h, H_PRED + 1)
                        elif ic >= 1:
                            s.wait_ge(s_vv.h, ffn[(tc, "h", ic - 1)])
                        s_ev.inc(s.activation(out=sgt[:, :], in_=bg[:, :], func=AF.Silu))
                        ffn[(tc, "sg", ic)] = s_ev.v

                @block.vector
                def _(v, tc=tc, spn=spn):
                    for ol in range(2):
                        ic = spn * 2 + ol
                        bu = pb[1 if ic % 2 == 0 else 3]
                        if tc >= 1 and spn == 0 and ol == 0:
                            v.wait_ge(s_mm.h, ffn[("dnmm", (tc * 2 - 1) * 4 + 3)])
                        v.wait_ge(s_ev.h, ffn[(tc, "sg", ic)])
                        v.wait_ge(s_mm.h, ffn[(tc, "umm", ic)])
                        v.tensor_tensor(hbuf[:, ic * FTC:(ic + 1) * FTC],
                                        sgt[:, :], bu[:, :], OP.mult)
                        s_vv.inc(v.engine_nop())
                        ffn[(tc, "h", ic)] = s_vv.v
            ffn[(tc, "gu_pe")] = s_mm.v
            if STAGE == 21 and tc == 0:
                if os.environ.get("KDSEL", "0") == "1":
                    finish(s_vv, ffn[(0, "h", 43)], [s3],
                           dumps=[wgu[0][:, :].bitcast(F32), wgu[1][:, :].bitcast(F32),
                                  x1fm[:, 0:8192].bitcast(F32)])
                else:
                    finish(s_vv, ffn[(0, "h", 43)], [s3],
                           dumps=[hbuf[:, 0:16384].bitcast(F32)])
                return nc

            for pr in range(2):
                pg = tc * 2 + pr
                for osp in range(4):
                    di = pg * 4 + osp

                    DN_PRED = s_mm.v

                    @block.sync
                    def _(sync, osp=osp, di=di, DN_PRED=DN_PRED):
                        if di == 0:
                            sync.wait_ge(s_dqst.h, DQ_ALL_STORES)
                        for qq in range(4):
                            if qq >= 2:
                                sync.wait_ge(s_mm.h, DN_PRED + qq - 1)
                            elif di >= 1:
                                sync.wait_ge(s_mm.h, ffn[("dn_q", di - 1, qq + 2)])
                            s_ld.inc(sync.dma_start(
                                out=wdsl[qq % 2][:, :].rearrange("p (k f) -> p k f", k=11),
                                in_=wdt[qq * 11:(qq + 1) * 11, :,
                                        osp * 512:(osp + 1) * 512].transpose([1, 0, 2])), 16)
                            ffn[("wd_ld", di, qq)] = s_ld.v

                    @block.tensor
                    def _(t, tc=tc, pr=pr, osp=osp, di=di):
                        for qq in range(4):
                            t.wait_ge(s_ld.h, ffn[("wd_ld", di, qq)])
                            t.wait_ge(s_vv.h, ffn[(tc, "h", qq * 11 + 10)])
                            if qq == 0 and di >= 1:
                                for ttl in range(2):
                                    t.wait_ge(s_ev.h, ffn[("yev", di - 1, ttl)])
                            for il in range(11):
                                ic = qq * 11 + il
                                for ttl in range(2):
                                    tl = pr * 2 + ttl
                                    bank = pb[4 + ttl]
                                    mi = t.matmul(bank[:, :],
                                                  hbuf[:, ic * FTC + tl * 128: ic * FTC + tl * 128 + 128],
                                                  wdsl[qq % 2][:, il * 512:(il + 1) * 512],
                                                  start=(ic == 0), stop=(ic == 43))
                            s_mm.inc(mi)
                            ffn[("dn_q", di, qq)] = s_mm.v
                        ffn[("dnmm", di)] = s_mm.v

                    @block.scalar
                    def _(s, pg=pg, osp=osp, di=di):
                        s.wait_ge(s_mm.h, ffn[("dnmm", di)])
                        for ttl in range(2):
                            bank = pb[4 + ttl]
                            if pg >= 1:
                                s.wait_ge(s_out.h, ffn[("ost", (pg - 1) * 2 + ttl)])
                            s_ev.inc(s.activation(out=y2[ttl][:, osp * 512:(osp + 1) * 512],
                                                  in_=bank[:, :], func=AF.Copy))
                            ffn[("yev", di, ttl)] = s_ev.v

                for ttl in range(2):
                    tt = tc * 4 + pr * 2 + ttl

                    @block.sync
                    def _(sync, tt=tt, ttl=ttl, tc=tc, pr=pr):
                        sync.wait_ge(s_st.h, ph5[(tt, "x1st")])
                        if tt >= 1:
                            sync.wait_ge(s_vv.h, ffn.get(("r2", tt - 1), 0))
                        s_ld.inc(sync.dma_start(out=xr2[ttl][:, :],
                                                in_=x1f[tt * 128:(tt + 1) * 128, :]), 16)
                        ffn[("xr2", tt)] = s_ld.v

                    @block.scalar
                    def _(s, tt=tt, ttl=ttl, tc=tc, pr=pr):
                        di_last = (tc * 2 + pr) * 4 + 3
                        s.wait_ge(s_ev.h, ffn[("yev", di_last, ttl)])
                        if tt == 15 and YDUMP:
                            s_st.inc(s.dma_start(out=dbg_h[:, 0:2048], in_=y2[ttl][:, :]), 16)
                            ffn[("ydmp", tt)] = s_st.v
                        s.activation(out=junk3[:, :], in_=y2[ttl][:, :], func=AF.Square,
                                     scale=1.0 / math.sqrt(D), accum_out=stats[:, 3:4])
                        s_ev.inc(s.drain())
                        ffn[("ysq", tt)] = s_ev.v

                    @block.vector
                    def _(v, tt=tt):
                        v.wait_ge(s_ev.h, ffn[("ysq", tt)])
                        v.tensor_copy(stats[:, 14:15], stats[:, 3:4])
                        s_vv.inc(v.engine_nop())
                        ffn[("msq", tt)] = s_vv.v

                    @block.scalar
                    def _(s, tt=tt):
                        s.wait_ge(s_vv.h, ffn[("msq", tt)])
                        s.activation(out=stats[:, 6:7], in_=stats[:, 14:15], func=AF.Ln,
                                     bias=ecol[:, 1:2], scale=1.0)
                        s_ev.inc(s.activation(out=stats[:, 7:8], in_=stats[:, 6:7], func=AF.Exp,
                                              scale=-0.5))
                        ffn[("rrms", tt)] = s_ev.v

                    @block.vector
                    def _(v, tt=tt, ttl=ttl):
                        v.wait_ge(s_ev.h, ffn[("rrms", tt)])
                        v.wait_ge(s_ld.h, ffn[("xr2", tt)])
                        if tt == 15 and YDUMP:
                            v.wait_ge(s_st.h, ffn[("ydmp", tt)])
                        for osp in range(4):
                            v.scalar_tensor_tensor(y2[ttl][:, osp * 512:(osp + 1) * 512],
                                                   y2[ttl][:, osp * 512:(osp + 1) * 512],
                                                   stats[:, 7:8],
                                                   xr2[ttl][:, osp * 512:(osp + 1) * 512],
                                                   OP.mult, OP.add)
                        s_vv.inc(v.engine_nop())
                        ffn[("r2", tt)] = s_vv.v

                    @block.scalar
                    def _(s, tt=tt, ttl=ttl):
                        s.wait_ge(s_vv.h, ffn[("r2", tt)])
                        s.activation(out=junk3[:, :], in_=y2[ttl][:, :], func=AF.Identity,
                                     scale=1.0 / D, accum_out=stats[:, 2:3])
                        s.activation(out=junk3[:, :], in_=y2[ttl][:, :], func=AF.Square,
                                     scale=1.0 / math.sqrt(D), accum_out=stats[:, 3:4])
                        s_ev.inc(s.drain())
                        ffn[("r2sq", tt)] = s_ev.v

                    @block.vector
                    def _(v, tt=tt):
                        v.wait_ge(s_ev.h, ffn[("r2sq", tt)])
                        v.scalar_tensor_tensor(stats[:, 5:6], stats[:, 2:3], stats[:, 2:3],
                                               stats[:, 3:4], OP.mult, OP.subtract)
                        s_vv.inc(v.engine_nop())
                        ffn[("var2", tt)] = s_vv.v

                    @block.scalar
                    def _(s, tt=tt):
                        s.wait_ge(s_vv.h, ffn[("var2", tt)])
                        s.activation(out=stats[:, 6:7], in_=stats[:, 5:6], func=AF.Ln,
                                     bias=ecol[:, 0:1], scale=-1.0)
                        s_ev.inc(s.activation(out=stats[:, 7:8], in_=stats[:, 6:7], func=AF.Exp,
                                              scale=-0.5))
                        ffn[("rstd2", tt)] = s_ev.v
                        if tt == 15 and YDUMP:
                            s.wait_ge(s_ev.h, ffn[("rstd2", tt)])
                            s_st.inc(s.dma_start(out=dbg_h[:, 2048:2064], in_=stats[:, :]), 16)

                    @block.vector
                    def _(v, tt=tt, ttl=ttl):
                        v.wait_ge(s_ev.h, ffn[("rstd2", tt)])
                        v.tensor_scalar(y2[ttl][:, :], y2[ttl][:, :], stats[:, 2:3], stats[:, 7:8],
                                        OP.subtract, OP.mult)
                        s_vv.inc(v.engine_nop())
                        ffn[("o", tt)] = s_vv.v

                    @block.scalar
                    def _(s, tt=tt, ttl=ttl):
                        s.wait_ge(s_vv.h, ffn[("o", tt)])
                        s_out.inc(s.dma_start(out=out_h[tt * 128:(tt + 1) * 128, :],
                                              in_=y2[ttl][:, :]), 16)
                    ffn[("ost", tt)] = s_out.v

        OST_ALL = s_out.v

        @block.sync
        def _(sync):
            sync.wait_ge(s_out.h, OST_ALL)
            if YDUMP:
                s_out.inc(sync.dma_start(out=dbg_h[:, 2064:4112],
                                         in_=hbuf[:, 0:4096].bitcast(F32)), 16)
                s_out.inc(sync.dma_start(out=dbg_h[:, 4112:6160],
                                         in_=x1fm[:, 0:4096].bitcast(F32)), 16)
                sync.wait_ge(s_out.h, s_out.v)
        s3.close()
    return nc


def _host_prep(inputs):
    x = np.ascontiguousarray(np.asarray(inputs["L0_attn_x"], dtype=np.float32)).reshape(16384, D)
    wqkv = np.ascontiguousarray(np.asarray(inputs["L0_attn_w_qkv"], dtype=np.float32))
    wproj = np.ascontiguousarray(np.asarray(inputs["L0_attn_w_proj"], dtype=np.float32))
    bproj = np.ascontiguousarray(np.asarray(inputs["L0_attn_b_proj"], dtype=np.float32)).reshape(1, D)
    gup = np.ascontiguousarray(np.asarray(inputs["L0_ffn_gate_up_packed"], dtype=np.int32))
    gus = np.ascontiguousarray(np.asarray(inputs["L0_ffn_gate_up_scales"], dtype=np.float16))
    dwn = np.ascontiguousarray(np.asarray(inputs["L0_ffn_down_packed"], dtype=np.int32))
    dws = np.ascontiguousarray(np.asarray(inputs["L0_ffn_down_scales"], dtype=np.float16))
    gup16 = gup.view(np.float16)
    dwn16 = dwn.view(np.float16)
    gsrep = np.ascontiguousarray(np.repeat(gus, 32, axis=0))
    dsrep = np.ascontiguousarray(np.repeat(dws, 32, axis=0))
    eye = np.eye(128, dtype=ml_dtypes.bfloat16)
    blk = (np.arange(128)[:, None] // 32 == np.arange(128)[None, :] // 32)
    maskbd = blk.astype(ml_dtypes.bfloat16)
    m4 = (np.arange(128)[:, None] // 32 == np.arange(4)[None, :]).astype(ml_dtypes.bfloat16)
    m4t = np.ascontiguousarray(m4.T)
    onesc = np.ones((128, 1), dtype=ml_dtypes.bfloat16)
    onesr = np.ones((1, 128), dtype=ml_dtypes.bfloat16)
    shared = dict(wqkv=wqkv, wproj=wproj, bproj=bproj, gup16=gup16, dwn16=dwn16,
                  gsrep=gsrep, dsrep=dsrep, eye=eye, maskbd=maskbd, mask4=m4,
                  mask4t=m4t, onesc=onesc, onesr=onesr)
    in_maps = []
    for c in range(8):
        m = dict(shared)
        m["x"] = np.ascontiguousarray(x[c * T:(c + 1) * T])
        in_maps.append(m)
    return in_maps


def kernel(**inputs):
    in_maps = _host_prep(inputs)
    nc = build()
    res = run_bass_kernel_spmd(nc, in_maps, core_ids=list(range(8)))
    out = np.concatenate([np.asarray(res.results[c]["out"]) for c in range(8)], axis=0)
    return out.reshape(4, 4096, D)


# version 13 (final)
# speedup vs baseline: 1.1739x; 1.0238x over previous
import math
import os
import numpy as np
import ml_dtypes
from contextlib import ExitStack

import concourse.bass as bass
import concourse.mybir as mybir
from concourse.bass_utils import run_bass_kernel_spmd

F32 = mybir.dt.float32
BF16 = mybir.dt.bfloat16
FP16 = mybir.dt.float16
U16 = mybir.dt.uint16
AF = mybir.ActivationFunctionType
OP = mybir.AluOpType
AX = mybir.AxisListType

T = 2048
D = 2048
I = 5632
GU = 2 * I
NKC = 16
NTT = 16
LN_EPS = 1e-5
RMS_EPS = 1e-6
ATT_EPS = 1e-15
FTC = 512
NFT = T // FTC
QSP = 256
NQS = D // QSP


class Sem:
    def __init__(self, nc, ctx, name):
        self.h = ctx.enter_context(nc.semaphore(name))
        self.v = 0

    def inc(self, inst, n=1):
        inst.then_inc(self.h, n)
        self.v += n
        return self.v


def build():
    STAGE = int(os.environ.get("KSTAGE", "99"))
    DQOFF = os.environ.get("KDQOFF", "") == "1"
    QLIM = int(os.environ.get("KQLIM", "8"))
    KVLIM = int(os.environ.get("KKVLIM", "8"))
    TTLIM = int(os.environ.get("KTTLIM", str(NTT)))
    QMODE = os.environ.get("KQMODE", "full")
    YDUMP = os.environ.get("KYDUMP", "") == "1"
    WDBG = os.environ.get("KWDBG", "") == "1"
    nc = bass.Bass("TRN2", target_bir_lowering=False, debug=False,
                   detect_race_conditions=(os.environ.get("KRACE", "1") == "1"))
    dt = nc.dram_tensor
    x_h = dt("x", [T, D], F32, kind="ExternalInput").ap()
    wqkv_h = dt("wqkv", [3 * D, D], F32, kind="ExternalInput").ap()
    wproj_h = dt("wproj", [D, D], F32, kind="ExternalInput").ap()
    bproj_h = dt("bproj", [1, D], F32, kind="ExternalInput").ap()
    gup16_h = dt("gup16", [GU, 512], FP16, kind="ExternalInput").ap()
    dwn16_h = dt("dwn16", [D, 1408], FP16, kind="ExternalInput").ap()
    gsrep_h = dt("gsrep", [512, GU], FP16, kind="ExternalInput").ap()
    dsrep_h = dt("dsrep", [1408, D], FP16, kind="ExternalInput").ap()
    eye_h = dt("eye", [128, 128], BF16, kind="ExternalInput").ap()
    maskbd_h = dt("maskbd", [128, 128], BF16, kind="ExternalInput").ap()
    mask4_h = dt("mask4", [128, 4], BF16, kind="ExternalInput").ap()
    mask4t_h = dt("mask4t", [4, 128], BF16, kind="ExternalInput").ap()
    onesc_h = dt("onesc", [128, 1], BF16, kind="ExternalInput").ap()
    onesr_h = dt("onesr", [1, 128], BF16, kind="ExternalInput").ap()
    xbf = dt("xbf", [T, D], BF16, kind="Internal").ap()
    wqkvbf = dt("wqkvbf", [3 * D, D], BF16, kind="Internal").ap()
    wprojbf = dt("wprojbf", [D, D], BF16, kind="Internal").ap()
    x1f = dt("x1f", [T, D], F32, kind="ExternalOutput" if WDBG else "Internal").ap()
    x1bf = dt("x1bf", [T, D], BF16,
              kind="ExternalOutput" if WDBG else "Internal").ap()
    wkind = "ExternalOutput" if WDBG else "Internal"
    wgt = dt("wgt", [16, 128, I], BF16, kind=wkind).ap()
    wup = dt("wup", [16, 128, I], BF16, kind=wkind).ap()
    wdt = dt("wdt", [44, 128, D], BF16, kind=wkind).ap()
    cin = dt("cin", [128, 2064], F32, kind="Internal").ap()
    cfull = dt("cfull", [128, 2064], F32, kind="Internal").ap()
    out_h = dt("out", [T, D], F32, kind="ExternalOutput").ap()
    dbg_h = dt("dbg", [128, 8192], F32, kind="ExternalOutput").ap()

    ctx = ExitStack()
    with ctx:
        def ten(st, name, shape, dtp):
            return st.enter_context(nc.sbuf_tensor(name, shape, dtp))
        eye = ten(ctx, "eye_t", [128, 128], BF16)
        maskbd = ten(ctx, "maskbd_t", [128, 128], BF16)
        mask4 = ten(ctx, "mask4_t", [128, 4], BF16)
        mask4t = ten(ctx, "mask4t_t", [4, 128], BF16)
        onesc = ten(ctx, "onesc_t", [128, 1], BF16)
        onesr = ten(ctx, "onesr_t", [1, 128], BF16)
        cconst = ten(ctx, "cconst", [128, 4], U16)
        ecol = ten(ctx, "ecol", [128, 4], F32)
        stats = ten(ctx, "stats", [128, 16], F32)
        dq_u = [ten(ctx, f"dq_u{i}", [128, 512], FP16) for i in range(4)]
        dq_s = [ten(ctx, f"dq_s{i}", [128, 512], FP16) for i in range(4)]
        dq_m = ten(ctx, "dq_m", [128, 512], U16)
        dq_t2 = ten(ctx, "dq_t2", [128, 512], U16)
        dq_sb = ten(ctx, "dq_sb", [128, 512], U16)
        dq_o = [ten(ctx, f"dq_o{i}", [128, 2048], BF16) for i in range(2)]
        vkf = ten(ctx, "vkf", [128, 2064], F32)
        pb = [ctx.enter_context(nc.psum_tensor(f"pb{i}", [128, 512], F32)) for i in range(6)]
        pT = ctx.enter_context(nc.psum_tensor("pT", [128, 128], BF16))
        pks = ctx.enter_context(nc.psum_tensor("pks", [128, 16], F32))

        s_ld = Sem(nc, ctx, "s_ld")
        s_cast = Sem(nc, ctx, "s_cast")
        s_dqst = Sem(nc, ctx, "s_dqst")
        s_dqv = Sem(nc, ctx, "s_dqv")
        s_mm = Sem(nc, ctx, "s_mm")
        s_ev = Sem(nc, ctx, "s_ev")
        s_vv = Sem(nc, ctx, "s_vv")
        s_st = Sem(nc, ctx, "s_st")
        s_cc = Sem(nc, ctx, "s_cc")
        s_out = Sem(nc, ctx, "s_out")

        s1a = ExitStack()
        q_fm = ten(s1a, "q_fm", [128, NKC * T], BF16)
        bias_bc = ten(s1a, "bias_bc", [128, D], F32)
        bprow = vkf[32:33, 0:D]
        bprow_b = vkf[0:1, :].bitcast(BF16)[:, 0:D]
        s1b = ExitStack()
        x_fm = ten(s1b, "x_fm", [128, NKC * T], BF16)
        wsl = [ten(s1b, f"wsl{i}", [128, NKC * QSP], BF16) for i in range(2)]
        wslB = [ten(s1b, f"wslb{i}", [128, NKC * QSP], BF16) for i in range(2)]
        kslab = ten(s1b, "kslab", [128, 2 * QSP], BF16)
        vtile = [ten(s1b, f"vtile{i}", [128, QSP], BF16) for i in range(2)]
        vkx = ten(s1b, "vkx", [128, 2064], F32)

        block = ctx.enter_context(nc.Block())

        def finish(sem, val, scopes, dumps=()):
            @block.sync
            def _(sync):
                sync.wait_ge(sem.h, val)
                off = 0
                for ap in dumps:
                    w = ap.shape[-1]
                    sync.dma_start(out=dbg_h[0:ap.shape[0], off:off + w], in_=ap).then_inc(s_out.h, 16)
                    s_out.v += 16
                    off += w
                sync.wait_ge(s_out.h, s_out.v)

            @block.gpsimd
            def _(g):
                g.wait_ge(s_dqst.h, s_dqst.v)
                g.wait_ge(s_cast.h, s_cast.v)
            for sc in scopes:
                sc.close()

        @block.sync
        def _(sync):
            for dst_t, src in [(eye[:, :], eye_h), (maskbd[:, :], maskbd_h), (mask4[:, :], mask4_h),
                               (mask4t[:, :], mask4t_h), (onesc[:, :], onesc_h), (onesr[:, :], onesr_h),
                               (bprow, bproj_h)]:
                s_ld.inc(sync.dma_start(out=dst_t, in_=src[:, :]), 16)
        P0_CONST = s_ld.v

        @block.gpsimd
        def _(g):
            for i in range(4):
                s_cast.inc(g.dma_start(out=xbf[i * 512:(i + 1) * 512, :],
                                       in_=x_h[i * 512:(i + 1) * 512, :]), 16)
        CAST_X = s_cast.v

        @block.vector
        def _(v):
            v.memset(cconst[:, 0:1], 252)
            v.memset(cconst[:, 1:2], 6)
            v.memset(ecol[:, 0:1], LN_EPS)
            v.memset(ecol[:, 1:2], RMS_EPS)
            v.memset(ecol[:, 2:3], ATT_EPS)
            s_vv.inc(v.engine_nop())

        @block.scalar
        def _(s):
            s.wait_ge(s_ld.h, P0_CONST)
            s_ev.inc(s.activation(out=bprow_b, in_=bprow, func=AF.Copy))
        BPROWB = s_ev.v

        @block.tensor
        def _(t):
            t.wait_ge(s_ev.h, BPROWB)
            for sp in range(4):
                s_mm.inc(t.matmul(pb[sp][:, :], onesr[:, :], bprow_b[0:1, sp * 512:(sp + 1) * 512],
                                  start=True, stop=True))
        BIAS_MM = s_mm.v

        @block.vector
        def _(v):
            v.wait_ge(s_mm.h, BIAS_MM)
            for sp in range(4):
                v.tensor_copy(bias_bc[:, sp * 512:(sp + 1) * 512], pb[sp][:, :])
            s_vv.inc(v.engine_nop())
        BIAS_BC = s_vv.v

        @block.gpsimd
        def _(g):
            for i in range(6):
                s_cast.inc(g.dma_start(out=wqkvbf[i * 1024:(i + 1) * 1024, :],
                                       in_=wqkv_h[i * 1024:(i + 1) * 1024, :]), 16)
            for i in range(2):
                s_cast.inc(g.dma_start(out=wprojbf[i * 1024:(i + 1) * 1024, :],
                                       in_=wproj_h[i * 1024:(i + 1) * 1024, :]), 16)
        CAST_QKV = [CAST_X + 16 * (i + 1) for i in range(6)]
        CAST_PROJ = s_cast.v

        @block.sync
        def _(sync):
            sync.wait_ge(s_cast.h, CAST_X)
            for kc in range(NKC):
                s_ld.inc(sync.dma_start(out=x_fm[:, kc * T:(kc + 1) * T],
                                        in_=xbf[:, kc * 128:(kc + 1) * 128],
                                        transpose=True), 16)
        XFM_DONE = s_ld.v
        if STAGE <= 1:
            finish(s_ld, XFM_DONE, [s1b, s1a],
                   dumps=[x_fm[:, 0:T].bitcast(F32), vkf[:, :]])
            return nc

        dq_work = []
        for p0 in range(11):
            for half in range(2):
                for wc in range(4):
                    dq_work.append(("gu", wc, half, p0, 1))
        for wc in range(11):
            for p0 in range(4):
                dq_work.append(("d", wc, 0, p0, 1))
        dq_items = []
        for piece in dq_work:
            dq_items.append(("slab",) + piece)
            for s_i in range(piece[4]):
                dq_items.append(("grp",) + piece + (s_i,))
        DQ_GATE_STORES = 88 * 16
        DQ_ALL_STORES = 132 * 16
        DQ_SPAN_THRESH = {}
        _cnt = 0
        for it in dq_items:
            if it[0] == "grp":
                _cnt += 16
                _, _kind, _wc, _half, _p0, _nsp, _si = it
                if _kind == "gu" and _half == 1 and _wc == 3 and _si == _nsp - 1:
                    for _spn in range(2 * _p0, 2 * (_p0 + _nsp)):
                        DQ_SPAN_THRESH[_spn] = _cnt
        dq_pos = [0]
        dq_ring = [0]
        dq_ngrp = [0]
        dq_slab_tgt = [0]
        dq_su = [0]
        dq_sguard = [0, 0, 0, 0]
        dq_cur = [0]

        def pump_dq(n):
            if DQOFF:
                return
            done = 0
            while done < n and dq_pos[0] < len(dq_items):
                item = dq_items[dq_pos[0]]
                if item[0] == "slab":
                    _, kind, wc, half, p0, nsp = item
                    ur = dq_su[0] & 3
                    dq_su[0] += 1
                    dq_cur[0] = ur
                    guard = dq_sguard[ur]

                    @block.sync
                    def _(sync, kind=kind, wc=wc, half=half, p0=p0, nsp=nsp,
                          guard=guard, ur=ur):
                        if guard:
                            sync.wait_ge(s_dqv.h, guard)
                        w = nsp * 512
                        if kind == "gu":
                            r0 = half * I + p0 * 512
                            s_ld.inc(sync.dma_start(out=dq_u[ur][:, 0:w],
                                                    in_=gup16_h[r0:r0 + w, wc * 128:(wc + 1) * 128],
                                                    transpose=True), 16)
                            s_ld.inc(sync.dma_start(out=dq_s[ur][:, 0:w],
                                                    in_=gsrep_h[wc * 128:(wc + 1) * 128, r0:r0 + w]), 16)
                        else:
                            r0 = p0 * 512
                            s_ld.inc(sync.dma_start(out=dq_u[ur][:, 0:w],
                                                    in_=dwn16_h[r0:r0 + w, wc * 128:(wc + 1) * 128],
                                                    transpose=True), 16)
                            s_ld.inc(sync.dma_start(out=dq_s[ur][:, 0:w],
                                                    in_=dsrep_h[wc * 128:(wc + 1) * 128, r0:r0 + w]), 16)
                    dq_slab_tgt[0] = s_ld.v
                    dq_pos[0] += 1
                else:
                    _, kind, wc, half, p0, nsp, s_i = item
                    ring = dq_ring[0]
                    dq_ring[0] ^= 1
                    tgt_ld = dq_slab_tgt[0]
                    ngrp = dq_ngrp[0]

                    ur = dq_cur[0]

                    @block.vector
                    def _(v, kind=kind, wc=wc, half=half, p0=p0, s_i=s_i,
                          ring=ring, tgt_ld=tgt_ld, ngrp=ngrp, ur=ur):
                        ot = dq_o[ring]
                        v.wait_ge(s_ld.h, tgt_ld)
                        if ngrp >= 2:
                            v.wait_ge(s_dqst.h, (ngrp - 1) * 16)
                        off = s_i * 512
                        u = dq_u[ur][:, off:off + 512].bitcast(U16)
                        for j in range(4):
                            if j == 0:
                                v.tensor_scalar(dq_m[:, :], u, 7, None, OP.bitwise_and)
                            else:
                                v.tensor_scalar(dq_m[:, :], u, 4 * j, 7,
                                                OP.logical_shift_right, OP.bitwise_and)
                            v.tensor_scalar(dq_t2[:, :], dq_m[:, :], 252, None, OP.mult)
                            v.scalar_tensor_tensor(dq_t2[:, :], dq_m[:, :], cconst[:, 0:1],
                                                   dq_t2[:, :], OP.add, OP.min)
                            if j == 3:
                                v.tensor_scalar(dq_sb[:, :], u, 32768, None, OP.bitwise_and)
                            else:
                                v.tensor_scalar(dq_sb[:, :], u, 12 - 4 * j, 32768,
                                                OP.logical_shift_left, OP.bitwise_and)
                            v.scalar_tensor_tensor(dq_sb[:, :], dq_t2[:, :], cconst[:, 1:2],
                                                   dq_sb[:, :], OP.logical_shift_left, OP.bitwise_or)
                            if kind == "d":
                                v.tensor_tensor(ot[:, j * 512:(j + 1) * 512],
                                                dq_sb[:, :].bitcast(BF16),
                                                dq_s[ur][:, off:off + 512], OP.mult)
                            else:
                                oap = ot[:, j * 512:(j + 1) * 512].rearrange("p (b a) -> p b a", b=4)
                                iap = dq_sb[:, :].bitcast(BF16).rearrange("p (a b) -> p b a", a=128)
                                sap = dq_s[ur][:, off:off + 512].rearrange("p (a b) -> p b a", a=128)
                                v.tensor_tensor(oap, iap, sap, OP.mult)
                        s_dqv.inc(v.engine_nop())
                    tgt_v = s_dqv.v
                    dq_sguard[ur] = tgt_v

                    @block.gpsimd
                    def _(g, kind=kind, wc=wc, half=half, p0=p0, s_i=s_i, ring=ring, tgt_v=tgt_v):
                        g.wait_ge(s_dqv.h, tgt_v)
                        ot = dq_o[ring]
                        osg = p0 + s_i
                        if kind == "d":
                            ic0 = wc * 4
                            dst = wdt[ic0:ic0 + 4, :, osg * 512:(osg + 1) * 512]
                        else:
                            tgt_t = wgt if half == 0 else wup
                            dst = tgt_t[wc:16:4, :, osg * 512:(osg + 1) * 512]
                        s_dqst.inc(g.dma_start(out=dst.transpose([1, 0, 2]),
                                               in_=ot[:, :].rearrange("p (j f) -> p j f", j=4)), 16)
                    dq_ngrp[0] += 1
                    dq_pos[0] += 1
                    done += 1

        Q_MM0 = s_mm.v
        Q_EV0 = s_ev.v
        for osp in range(min(NQS, QLIM)):
            buf = wsl[osp % 2]

            @block.sync
            def _(sync, osp=osp, buf=buf):
                if osp == 0:
                    sync.wait_ge(s_ld.h, XFM_DONE)
                if osp >= 2:
                    sync.wait_ge(s_mm.h, Q_MM0 + (osp - 1) * 8)
                sync.wait_ge(s_cast.h, CAST_QKV[min(5, ((osp + 1) * QSP - 1) // 1024)])
                for kc in range(NKC):
                    s_ld.inc(sync.dma_start(out=buf[:, kc * QSP:(kc + 1) * QSP],
                                            in_=wqkvbf[osp * QSP:(osp + 1) * QSP,
                                                       kc * 128:(kc + 1) * 128],
                                            transpose=True), 16)
            Q_LD = s_ld.v

            if QMODE != "ldonly":
                @block.tensor
                def _(t, osp=osp, buf=buf, Q_LD=Q_LD):
                    t.wait_ge(s_ld.h, Q_LD)
                    if osp == 0:
                        t.wait_ge(s_vv.h, BIAS_BC)
                    for och in range(2):
                        for tsp in range(4):
                            gidx = osp * 8 + och * 4 + tsp
                            bank = pb[gidx % 2]
                            if gidx >= 2 and QMODE == "full":
                                t.wait_ge(s_ev.h, Q_EV0 + gidx - 1)
                            for kc in range(NKC):
                                mi = t.matmul(bank[:, :],
                                              buf[:, kc * QSP + och * 128: kc * QSP + (och + 1) * 128],
                                              x_fm[:, kc * T + tsp * 512: kc * T + (tsp + 1) * 512],
                                              start=(kc == 0), stop=(kc == NKC - 1))
                            s_mm.inc(mi)

            if QMODE == "full":
                @block.scalar
                def _(s, osp=osp):
                    for och in range(2):
                        for tsp in range(4):
                            gidx = osp * 8 + och * 4 + tsp
                            bank = pb[gidx % 2]
                            s.wait_ge(s_mm.h, Q_MM0 + gidx + 1)
                            oc = osp * 2 + och
                            s_ev.inc(s.activation(
                                out=q_fm[:, oc * T + tsp * 512: oc * T + (tsp + 1) * 512],
                                in_=bank[:, :], func=AF.Relu))
            pump_dq(8)
        Q_PE_DONE = Q_MM0 + min(NQS, QLIM) * 8
        if STAGE == 12:
            @block.scalar
            def _(s):
                pass
            finish(s_ev, s_ev.v, [s1b, s1a])
            return nc

        kv_mm = {}
        kv_ev = {}
        KV_MM0 = s_mm.v
        for sp in range(min(NQS, KVLIM)):
            bufk = wsl[0] if sp % 2 == 0 else wslB[0]
            bufv = wsl[1] if sp % 2 == 0 else wslB[1]
            vk_bank = pb[4 + (sp % 2)]

            @block.sync
            def _(sync, sp=sp, bufk=bufk, bufv=bufv):
                if sp == 0:
                    sync.wait_ge(s_cast.h, CAST_QKV[5])
                    sync.wait_ge(s_mm.h, Q_PE_DONE)
                elif sp == 1:
                    sync.wait_ge(s_cast.h, CAST_QKV[5])
                else:
                    sync.wait_ge(s_mm.h, kv_mm[(sp - 2, "pe_done")])
                for kc in range(NKC):
                    s_ld.inc(sync.dma_start(out=bufk[:, kc * QSP:(kc + 1) * QSP],
                                            in_=wqkvbf[2048 + sp * QSP: 2048 + (sp + 1) * QSP,
                                                       kc * 128:(kc + 1) * 128],
                                            transpose=True), 16)
                for kc in range(NKC):
                    s_ld.inc(sync.dma_start(out=bufv[:, kc * QSP:(kc + 1) * QSP],
                                            in_=wqkvbf[4096 + sp * QSP: 4096 + (sp + 1) * QSP,
                                                       kc * 128:(kc + 1) * 128],
                                            transpose=True), 16)
            kv_mm[(sp, "wld")] = s_ld.v
            if STAGE == 17 and sp == 0:
                finish(s_ld, kv_mm[(0, "wld")], [s1b, s1a],
                       dumps=[bufk[:, :].bitcast(F32), bufv[:, :].bitcast(F32)])
                return nc
            ev0 = s_ev.v
            kv_ev[sp] = ev0
            mm0 = s_mm.v
            kv_mm[(sp, "mm0")] = mm0

            @block.tensor
            def _(t, sp=sp, bufk=bufk, bufv=bufv, vk_bank=vk_bank, ev0=ev0, mm0=mm0):
                t.wait_ge(s_ld.h, kv_mm[(sp, "wld")])
                if sp >= 1:
                    t.wait_ge(s_ev.h, kv_mm[(sp - 1, "ev_done")])
                if sp >= 2:
                    t.wait_ge(s_vv.h, kv_mm[(sp - 2, "vkev")])
                for tt in range(TTLIM):
                    if tt >= 1:
                        t.wait_ge(s_ev.h, ev0 + 2 * (tt - 1) + 1)
                    for kc in range(NKC):
                        mi = t.matmul(pb[2][:, 0:QSP],
                                      x_fm[:, kc * T + tt * 128: kc * T + (tt + 1) * 128],
                                      bufk[:, kc * QSP:(kc + 1) * QSP],
                                      start=(kc == 0), stop=(kc == NKC - 1))
                    s_mm.inc(mi)
                    if tt >= 1:
                        t.wait_ge(s_ev.h, ev0 + 2 * (tt - 1) + 2)
                    for kc in range(NKC):
                        mi = t.matmul(pb[3][:, 0:QSP],
                                      x_fm[:, kc * T + tt * 128: kc * T + (tt + 1) * 128],
                                      bufv[:, kc * QSP:(kc + 1) * QSP],
                                      start=(kc == 0), stop=(kc == NKC - 1))
                    s_mm.inc(mi)
                    if os.environ.get("KNOVK", "") == "1":
                        s_mm.inc(mi)
                        continue
                    t.wait_ge(s_ev.h, ev0 + 2 * tt + 2)
                    vb = vtile[tt % 2]
                    for cl in range(2):
                        t.matmul(vk_bank[:, cl * 128:(cl + 1) * 128],
                                 vb[:, cl * 128:(cl + 1) * 128],
                                 kslab[:, (tt % 2) * QSP + cl * 128: (tt % 2) * QSP + (cl + 1) * 128],
                                 start=(tt == 0 and cl == 0), stop=(tt == TTLIM - 1 and cl == 1))
                        mi = t.matmul(pks[:, (sp * 2 + cl):(sp * 2 + cl) + 1],
                                      kslab[:, (tt % 2) * QSP + cl * 128: (tt % 2) * QSP + (cl + 1) * 128],
                                      onesc[:, :],
                                      start=(sp == 0 and tt == 0 and cl == 0),
                                      stop=(sp == min(NQS, KVLIM) - 1 and tt == TTLIM - 1 and cl == 1))
                    s_mm.inc(mi)
            kv_mm[(sp, "pe_done")] = s_mm.v

            @block.scalar
            def _(s, sp=sp, mm0=mm0):
                if os.environ.get("KNOKEV", "") == "1":
                    return
                for tt in range(TTLIM):
                    s.wait_ge(s_mm.h, mm0 + 3 * tt + 1)
                    if tt >= 2:
                        s.wait_ge(s_mm.h, mm0 + 3 * (tt - 2) + 3)
                    s_ev.inc(s.activation(out=kslab[:, (tt % 2) * QSP:(tt % 2 + 1) * QSP],
                                          in_=pb[2][:, 0:QSP], func=AF.Relu))
                    s.wait_ge(s_mm.h, mm0 + 3 * tt + 2)
                    s_ev.inc(s.activation(out=vtile[tt % 2][:, :],
                                          in_=pb[3][:, 0:QSP], func=AF.Copy))
            kv_mm[(sp, "ev_done")] = s_ev.v

            @block.vector
            def _(v, sp=sp, vk_bank=vk_bank):
                v.wait_ge(s_mm.h, kv_mm[(sp, "pe_done")])
                v.tensor_copy(vkx[:, sp * QSP:(sp + 1) * QSP], vk_bank[:, 0:QSP])
                s_vv.inc(v.engine_nop())
            kv_mm[(sp, "vkev")] = s_vv.v
            pump_dq(8)

        @block.vector
        def _(v):
            v.wait_ge(s_mm.h, kv_mm[(min(NQS, KVLIM) - 1, "pe_done")])
            v.tensor_copy(vkx[:, 2048:2064], pks[:, :])
            s_vv.inc(v.engine_nop())
        VKX_DONE = s_vv.v
        if STAGE <= 2:
            finish(s_vv, VKX_DONE, [s1b, s1a],
                   dumps=() if os.environ.get("KNODMP", "") == "1" else (
                          vkx[:, :],
                          q_fm[:, 0:T].bitcast(F32),
                          x_fm[:, 0:T].bitcast(F32),
                          kslab[:, :].bitcast(F32)))
            return nc

        @block.scalar
        def _(s):
            s.wait_ge(s_vv.h, VKX_DONE)
            s_st.inc(s.dma_start(out=cin[:, :], in_=vkx[:, :]), 16)
        CIN_ST = s_st.v

        @block.gpsimd
        def _(g):
            g.wait_ge(s_st.h, CIN_ST)
            s_cc.inc(g.collective_compute(
                "AllReduce", OP.add,
                replica_groups=[[0, 1], [2, 3], [4, 5], [6, 7]],
                ins=[cin[:, :].opt()],
                outs=[cfull[:, :].opt()]), 1)

        @block.sync
        def _(sync):
            sync.wait_ge(s_cc.h, s_cc.v)
            s_ld.inc(sync.dma_start(out=vkf[:, :], in_=cfull[:, :]), 16)
        VKF_LD = s_ld.v
        if STAGE <= 3:
            finish(s_ld, VKF_LD, [s1b, s1a])
            return nc

        pump_dq(8)
        s1b.close()
        s2 = ExitStack()
        wpT = ten(s2, "wpT", [128, NKC * 2048], BF16)
        rden = ten(s2, "rden", [128, 2048], BF16)
        xres = [ten(s2, f"xres{i}", [128, D], F32) for i in range(2)]
        rbuf = ten(s2, "rbuf", [128, D], F32)
        x1b = [ten(s2, "x1b0", [128, D], BF16)] * 2
        tmp128 = ten(s2, "tmp128", [128, 128], BF16)
        Abuf = [ten(s2, f"Abuf{i}", [128, 128], BF16) for i in range(2)]
        cbuf = ten(s2, "cbuf", [128, 4], BF16)
        den4b = ten(s2, "den4b", [4, 2048], BF16)

        @block.sync
        def _(sync):
            sync.wait_ge(s_cast.h, CAST_PROJ)
            sync.wait_ge(s_mm.h, kv_mm[(NQS - 1, "pe_done")])
            sync.wait_ge(s_st.h, CIN_ST)
            for kc in range(NKC):
                for osp in range(4):
                    s_ld.inc(sync.dma_start(
                        out=wpT[:, kc * 2048 + osp * 512: kc * 2048 + (osp + 1) * 512],
                        in_=wprojbf[osp * 512:(osp + 1) * 512, kc * 128:(kc + 1) * 128],
                        transpose=True), 16)
        WPT_DONE = s_ld.v

        ph4 = {}
        for c in range(NKC):
            @block.vector
            def _(v, c=c):
                if c == 0:
                    v.wait_ge(s_ld.h, VKF_LD)
                if c >= 1:
                    v.wait_ge(s_mm.h, ph4[(c - 1, "denmm", 3)])
                v.tensor_tensor(tmp128[:, :], vkf[:, c * 128:(c + 1) * 128], maskbd[:, :], OP.mult)
                v.tensor_scalar(cbuf[:, :], mask4[:, :], vkf[:, 2048 + c:2049 + c], None, OP.mult)
                s_vv.inc(v.engine_nop())
            ph4[(c, "prep")] = s_vv.v

            @block.tensor
            def _(t, c=c):
                t.wait_ge(s_vv.h, ph4[(c, "prep")])
                if c >= 1:
                    t.wait_ge(s_ev.h, ph4[(c - 1, "Aev")])
                s_mm.inc(t.transpose(pT[:, :], tmp128[:, :], eye[:, :]))
            ph4[(c, "tr")] = s_mm.v

            @block.scalar
            def _(s, c=c):
                s.wait_ge(s_mm.h, ph4[(c, "tr")])
                s_ev.inc(s.activation(out=Abuf[c % 2][:, :], in_=pT[:, :], func=AF.Copy))
            ph4[(c, "Aev")] = s_ev.v

            den_mm = []
            den_ev = []
            dbc_mm = []
            dbc_ev = []

            DEN_PRED = s_vv.v

            @block.tensor
            def _(t, c=c, DEN_PRED=DEN_PRED):
                t.wait_ge(s_ev.h, ph4[(c, "Aev")])
                for tsp in range(4):
                    bank = pb[2 + tsp % 2]
                    if tsp >= 2:
                        t.wait_ge(s_vv.h, DEN_PRED + tsp - 1)
                    elif c >= 1:
                        t.wait_ge(s_ev.h, ph4[(c - 1, "dbcev", 2 + tsp)])
                    s_mm.inc(t.matmul(bank[0:4, :], cbuf[:, :],
                             q_fm[:, c * T + tsp * 512: c * T + (tsp + 1) * 512],
                             start=True, stop=True))
                    ph4[(c, "denmm", tsp)] = s_mm.v

            @block.vector
            def _(v, c=c):
                for tsp in range(4):
                    bank = pb[2 + tsp % 2]
                    v.wait_ge(s_mm.h, ph4[(c, "denmm", tsp)])
                    v.tensor_scalar(den4b[:, tsp * 512:(tsp + 1) * 512], bank[0:4, :],
                                    ecol[0:4, 2:3], None, OP.add)
                    s_vv.inc(v.engine_nop())
                    ph4[(c, "denev", tsp)] = s_vv.v

            OUTEV_PRED = s_vv.v

            @block.tensor
            def _(t, c=c, OUTEV_PRED=OUTEV_PRED):
                for tsp in range(4):
                    bank = pb[2 + tsp % 2]
                    t.wait_ge(s_vv.h, ph4[(c, "denev", tsp + 2 if tsp < 2 else 3)])
                    if tsp >= 2:
                        t.wait_ge(s_ev.h, ph4[(c, "Aev")] + 2 * (tsp - 1))
                    s_mm.inc(t.matmul(bank[:, :], mask4t[:, :],
                             den4b[:, tsp * 512:(tsp + 1) * 512], start=True, stop=True))
                    ph4[(c, "dbcmm", tsp)] = s_mm.v
                for tsp in range(4):
                    bank = pb[tsp % 2]
                    if tsp >= 2:
                        t.wait_ge(s_vv.h, OUTEV_PRED + tsp - 1)
                    elif c >= 1:
                        t.wait_ge(s_vv.h, ph4[(c - 1, "outev", tsp + 2)])
                    s_mm.inc(t.matmul(bank[:, :], Abuf[c % 2][:, :],
                             q_fm[:, c * T + tsp * 512: c * T + (tsp + 1) * 512],
                             start=True, stop=True))
                    ph4[(c, "outmm", tsp)] = s_mm.v

            @block.scalar
            def _(s, c=c):
                for tsp in range(4):
                    bank = pb[2 + tsp % 2]
                    s.wait_ge(s_mm.h, ph4[(c, "dbcmm", tsp)])
                    s_ev.inc(s.activation(out=rden[:, tsp * 512:(tsp + 1) * 512],
                                          in_=bank[:, :], func=AF.Ln))
                    s_ev.inc(s.activation(out=rden[:, tsp * 512:(tsp + 1) * 512],
                                          in_=rden[:, tsp * 512:(tsp + 1) * 512],
                                          func=AF.Exp, scale=-1.0))
                    ph4[(c, "dbcev", tsp)] = s_ev.v

            @block.vector
            def _(v, c=c):
                for tsp in range(4):
                    bank = pb[tsp % 2]
                    v.wait_ge(s_mm.h, ph4[(c, "outmm", tsp)])
                    v.wait_ge(s_ev.h, ph4[(c, "dbcev", tsp)])
                    v.tensor_tensor(q_fm[:, c * T + tsp * 512: c * T + (tsp + 1) * 512],
                                    bank[:, :], rden[:, tsp * 512:(tsp + 1) * 512], OP.mult)
                    s_vv.inc(v.engine_nop())
                    ph4[(c, "outev", tsp)] = s_vv.v
            if c % 2 == 1:
                pump_dq(2)
        P4_DONE_VV = s_vv.v
        if STAGE <= 4:
            finish(s_vv, P4_DONE_VV, [s2, s1a],
                   dumps=[q_fm[:, 0:T].bitcast(F32), rden[:, :].bitcast(F32), vkf[:, :]])
            return nc

        ph5 = {}
        for tt in range(NTT):
            @block.sync
            def _(sync, tt=tt):
                if tt == 0:
                    sync.wait_ge(s_ld.h, WPT_DONE)
                    sync.wait_ge(s_vv.h, P4_DONE_VV)
                if tt >= 2:
                    sync.wait_ge(s_vv.h, ph5[(tt - 2, "rsum")])
                s_ld.inc(sync.dma_start(out=xres[tt % 2][:, :],
                                        in_=x_h[tt * 128:(tt + 1) * 128, :]), 16)
            ph5[(tt, "xld")] = s_ld.v

            @block.vector
            def _(v, tt=tt):
                v.wait_ge(s_ld.h, ph5[(tt, "xld")])
                v.tensor_tensor(xres[tt % 2][:, :], xres[tt % 2][:, :], bias_bc[:, :], OP.add)
                s_vv.inc(v.engine_nop())
                ph5[(tt, "xb")] = s_vv.v

            @block.tensor
            def _(t, tt=tt):
                if tt == 0:
                    t.wait_ge(s_vv.h, P4_DONE_VV)
                    t.wait_ge(s_ev.h, ph4[(NKC - 1, "dbcev", 3)])
                for osp in range(4):
                    gidx = tt * 4 + osp
                    bank = pb[gidx % 4]
                    if gidx >= 4:
                        t.wait_ge(s_vv.h, ph5[(gidx - 4, "prjev")])
                    for kc in range(NKC):
                        mi = t.matmul(bank[:, :],
                                      q_fm[:, kc * T + tt * 128: kc * T + (tt + 1) * 128],
                                      wpT[:, kc * 2048 + osp * 512: kc * 2048 + (osp + 1) * 512],
                                      start=(kc == 0), stop=(kc == NKC - 1))
                    s_mm.inc(mi)
                    ph5[(gidx, "prjmm")] = s_mm.v

            @block.vector
            def _(v, tt=tt):
                v.wait_ge(s_vv.h, ph5[(tt, "xb")])
                if tt >= 1:
                    v.wait_ge(s_st.h, ph5[(tt - 1, "x1st")])
                    v.wait_ge(s_ev.h, ph5[(tt - 1, "x1b")])
                for osp in range(4):
                    gidx = tt * 4 + osp
                    bank = pb[gidx % 4]
                    v.wait_ge(s_mm.h, ph5[(gidx, "prjmm")])
                    v.scalar_tensor_tensor(rbuf[:, osp * 512:(osp + 1) * 512],
                                           bank[:, :], 1.0,
                                           xres[tt % 2][:, osp * 512:(osp + 1) * 512],
                                           OP.bypass, OP.add)
                    s_vv.inc(v.engine_nop())
                    ph5[(gidx, "prjev")] = s_vv.v
                ph5[(tt, "rsum")] = s_vv.v

            @block.scalar
            def _(s, tt=tt):
                s.wait_ge(s_vv.h, ph5[(tt, "rsum")])
                s.activation(out=rden[:, :], in_=rbuf[:, :], func=AF.Identity,
                             scale=1.0 / D, accum_out=stats[:, 2:3])
                s.activation(out=rden[:, :], in_=rbuf[:, :], func=AF.Square,
                             scale=1.0 / math.sqrt(D), accum_out=stats[:, 3:4])
                s_ev.inc(s.drain())
                ph5[(tt, "sq")] = s_ev.v

            @block.vector
            def _(v, tt=tt):
                v.wait_ge(s_ev.h, ph5[(tt, "sq")])
                v.scalar_tensor_tensor(stats[:, 5:6], stats[:, 2:3], stats[:, 2:3],
                                       stats[:, 3:4], OP.mult, OP.subtract)
                s_vv.inc(v.engine_nop())
                ph5[(tt, "var")] = s_vv.v

            @block.scalar
            def _(s, tt=tt):
                s.wait_ge(s_vv.h, ph5[(tt, "var")])
                s.activation(out=stats[:, 6:7], in_=stats[:, 5:6], func=AF.Ln,
                             bias=ecol[:, 0:1], scale=-1.0)
                s_ev.inc(s.activation(out=stats[:, 7:8], in_=stats[:, 6:7], func=AF.Exp,
                                      scale=-0.5))
                ph5[(tt, "rstd")] = s_ev.v

            @block.vector
            def _(v, tt=tt):
                v.wait_ge(s_ev.h, ph5[(tt, "rstd")])
                v.tensor_scalar(rbuf[:, :], rbuf[:, :], stats[:, 2:3], stats[:, 7:8],
                                OP.subtract, OP.mult)
                s_vv.inc(v.engine_nop())
                ph5[(tt, "x1")] = s_vv.v

            @block.scalar
            def _(s, tt=tt):
                s.wait_ge(s_vv.h, ph5[(tt, "x1")])
                if tt >= 1:
                    s.wait_ge(s_st.h, ph5[(tt - 1, "x1st")])
                for j in range(4):
                    ai = s.activation(out=x1b[tt % 2][:, j * 512:(j + 1) * 512],
                                      in_=rbuf[:, j:D:4], func=AF.Copy)
                s_ev.inc(ai)
                ph5[(tt, "x1b")] = s_ev.v
                s_st.inc(s.dma_start(out=x1f[tt * 128:(tt + 1) * 128, :], in_=rbuf[:, :]), 16)
                s.wait_ge(s_ev.h, ph5[(tt, "x1b")])
                s_st.inc(s.dma_start(out=x1bf[tt * 128:(tt + 1) * 128, :], in_=x1b[tt % 2][:, :]), 16)
            ph5[(tt, "x1st")] = s_st.v

            pump_dq(4)
        P5_PE_DONE = ph5[(NTT * 4 - 1, "prjmm")]
        P5_ST_DONE = ph5[(NTT - 1, "x1st")]
        P5_EV_DONE = ph5[(NTT - 1, "x1b")]
        P5_VV_DONE = s_vv.v
        if STAGE <= 5:
            finish(s_st, P5_ST_DONE, [s2, s1a],
                   dumps=[rbuf[:, :], stats[:, :]])
            return nc

        s2.close()
        s1a.close()
        s3 = ExitStack()
        x1fm = ten(s3, "x1fm", [128, NKC * FTC], BF16)
        wgu = [ten(s3, f"wgu{i}", [128, NKC * 256], BF16) for i in range(2)]
        wub = [ten(s3, f"wub{i}", [128, NKC * 256], BF16) for i in range(2)]
        hbuf = ten(s3, "hbuf", [128, 44 * FTC], BF16)
        sgt = ten(s3, "sgt", [128, FTC], BF16)
        wdsl = [ten(s3, f"wdsl{i}", [128, 11 * 512], BF16) for i in range(2)]
        y2 = [ten(s3, f"y2_{i}", [128, D], F32) for i in range(2)]
        xr2 = [ten(s3, "xr2_0", [128, D], F32)] * 2
        junk3 = ten(s3, "junk3", [128, D], BF16)

        ffn = {}
        S3_GUARD_EMITTED = [False]
        for tc in range(NFT):
            @block.sync
            def _(sync, tc=tc):
                if not S3_GUARD_EMITTED[0]:
                    sync.wait_ge(s_mm.h, P5_PE_DONE)
                    sync.wait_ge(s_st.h, P5_ST_DONE)
                    sync.wait_ge(s_ev.h, P5_EV_DONE)
                    sync.wait_ge(s_vv.h, P5_VV_DONE)
                    S3_GUARD_EMITTED[0] = True
                sync.wait_ge(s_st.h, ph5[(tc * 4 + 3, "x1st")])
                if tc >= 1:
                    sync.wait_ge(s_mm.h, ffn[(tc - 1, "gu_pe")])
                for kc in range(NKC):
                    s_ld.inc(sync.dma_start(out=x1fm[:, kc * FTC:(kc + 1) * FTC],
                                            in_=x1bf[tc * FTC:(tc + 1) * FTC,
                                                     kc * 128:(kc + 1) * 128],
                                            transpose=True), 16)
            ffn[(tc, "x1fm")] = s_ld.v

            GSPN = 22
            for spn in range(GSPN):
                @block.sync
                def _(sync, tc=tc, spn=spn):
                    gi = tc * GSPN + spn
                    if tc == 0 and not DQOFF:
                        sync.wait_ge(s_dqst.h, DQ_SPAN_THRESH[spn])
                    if gi >= 2:
                        sync.wait_ge(s_mm.h, ffn[(gi - 2, "gu_sp_pe")])
                    ring = spn % 2
                    s_ld.inc(sync.dma_start(
                        out=wgu[ring][:, :].rearrange("p (k f) -> p k f", k=16),
                        in_=wgt[0:16, :, spn * 256:(spn + 1) * 256].transpose([1, 0, 2])), 16)
                    s_ld.inc(sync.dma_start(
                        out=wub[ring][:, :].rearrange("p (k f) -> p k f", k=16),
                        in_=wup[0:16, :, spn * 256:(spn + 1) * 256].transpose([1, 0, 2])), 16)
                ffn[(tc * GSPN + spn, "gu_ld")] = s_ld.v

                @block.tensor
                def _(t, tc=tc, spn=spn):
                    gi = tc * GSPN + spn
                    t.wait_ge(s_ld.h, ffn[(gi, "gu_ld")])
                    if spn == 0:
                        t.wait_ge(s_ld.h, ffn[(tc, "x1fm")])
                    ring = spn % 2
                    for ol in range(2):
                        ic = spn * 2 + ol
                        bg = pb[0 if ic % 2 == 0 else 2]
                        bu = pb[1 if ic % 2 == 0 else 3]
                        if ic >= 2:
                            t.wait_ge(s_vv.h, ffn[(tc, "h", ic - 2)])
                        for kc in range(NKC):
                            mi = t.matmul(bg[:, :],
                                          wgu[ring][:, kc * 256 + ol * 128: kc * 256 + (ol + 1) * 128],
                                          x1fm[:, kc * FTC:(kc + 1) * FTC],
                                          start=(kc == 0), stop=(kc == NKC - 1))
                        s_mm.inc(mi)
                        ffn[(tc, "gmm", ic)] = s_mm.v
                        for kc in range(NKC):
                            mi = t.matmul(bu[:, :],
                                          wub[ring][:, kc * 256 + ol * 128: kc * 256 + (ol + 1) * 128],
                                          x1fm[:, kc * FTC:(kc + 1) * FTC],
                                          start=(kc == 0), stop=(kc == NKC - 1))
                        s_mm.inc(mi)
                        ffn[(tc, "umm", ic)] = s_mm.v
                ffn[(tc * GSPN + spn, "gu_sp_pe")] = s_mm.v

                H_PRED = s_vv.v

                @block.scalar
                def _(s, tc=tc, spn=spn, H_PRED=H_PRED):
                    for ol in range(2):
                        ic = spn * 2 + ol
                        bg = pb[0 if ic % 2 == 0 else 2]
                        s.wait_ge(s_mm.h, ffn[(tc, "gmm", ic)])
                        if ol == 1:
                            s.wait_ge(s_vv.h, H_PRED + 1)
                        elif ic >= 1:
                            s.wait_ge(s_vv.h, ffn[(tc, "h", ic - 1)])
                        s_ev.inc(s.activation(out=sgt[:, :], in_=bg[:, :], func=AF.Silu))
                        ffn[(tc, "sg", ic)] = s_ev.v

                @block.vector
                def _(v, tc=tc, spn=spn):
                    for ol in range(2):
                        ic = spn * 2 + ol
                        bu = pb[1 if ic % 2 == 0 else 3]
                        if tc >= 1 and spn == 0 and ol == 0:
                            v.wait_ge(s_mm.h, ffn[("dnmm", (tc * 2 - 1) * 4 + 3)])
                        v.wait_ge(s_ev.h, ffn[(tc, "sg", ic)])
                        v.wait_ge(s_mm.h, ffn[(tc, "umm", ic)])
                        v.tensor_tensor(hbuf[:, ic * FTC:(ic + 1) * FTC],
                                        sgt[:, :], bu[:, :], OP.mult)
                        s_vv.inc(v.engine_nop())
                        ffn[(tc, "h", ic)] = s_vv.v
                if tc == 0:
                    pump_dq(2)
            ffn[(tc, "gu_pe")] = s_mm.v
            if STAGE == 21 and tc == 0:
                if os.environ.get("KDSEL", "0") == "1":
                    finish(s_vv, ffn[(0, "h", 43)], [s3],
                           dumps=[wgu[0][:, :].bitcast(F32), wgu[1][:, :].bitcast(F32),
                                  x1fm[:, 0:8192].bitcast(F32)])
                else:
                    finish(s_vv, ffn[(0, "h", 43)], [s3],
                           dumps=[hbuf[:, 0:16384].bitcast(F32)])
                return nc

            if tc == 0:
                pump_dq(10 ** 9)

            for pr in range(2):
                pg = tc * 2 + pr
                for osp in range(4):
                    di = pg * 4 + osp

                    DN_PRED = s_mm.v

                    @block.sync
                    def _(sync, osp=osp, di=di, DN_PRED=DN_PRED):
                        if di == 0:
                            sync.wait_ge(s_dqst.h, DQ_ALL_STORES)
                        for qq in range(4):
                            if qq >= 2:
                                sync.wait_ge(s_mm.h, DN_PRED + qq - 1)
                            elif di >= 1:
                                sync.wait_ge(s_mm.h, ffn[("dn_q", di - 1, qq + 2)])
                            s_ld.inc(sync.dma_start(
                                out=wdsl[qq % 2][:, :].rearrange("p (k f) -> p k f", k=11),
                                in_=wdt[qq * 11:(qq + 1) * 11, :,
                                        osp * 512:(osp + 1) * 512].transpose([1, 0, 2])), 16)
                            ffn[("wd_ld", di, qq)] = s_ld.v

                    @block.tensor
                    def _(t, tc=tc, pr=pr, osp=osp, di=di):
                        for qq in range(4):
                            t.wait_ge(s_ld.h, ffn[("wd_ld", di, qq)])
                            t.wait_ge(s_vv.h, ffn[(tc, "h", qq * 11 + 10)])
                            if qq == 0 and di >= 1:
                                for ttl in range(2):
                                    t.wait_ge(s_ev.h, ffn[("yev", di - 1, ttl)])
                            for il in range(11):
                                ic = qq * 11 + il
                                for ttl in range(2):
                                    tl = pr * 2 + ttl
                                    bank = pb[4 + ttl]
                                    mi = t.matmul(bank[:, :],
                                                  hbuf[:, ic * FTC + tl * 128: ic * FTC + tl * 128 + 128],
                                                  wdsl[qq % 2][:, il * 512:(il + 1) * 512],
                                                  start=(ic == 0), stop=(ic == 43))
                            s_mm.inc(mi)
                            ffn[("dn_q", di, qq)] = s_mm.v
                        ffn[("dnmm", di)] = s_mm.v

                    @block.scalar
                    def _(s, pg=pg, osp=osp, di=di):
                        s.wait_ge(s_mm.h, ffn[("dnmm", di)])
                        for ttl in range(2):
                            bank = pb[4 + ttl]
                            if pg >= 1:
                                s.wait_ge(s_out.h, ffn[("ost", (pg - 1) * 2 + ttl)])
                            s_ev.inc(s.activation(out=y2[ttl][:, osp * 512:(osp + 1) * 512],
                                                  in_=bank[:, :], func=AF.Copy))
                            ffn[("yev", di, ttl)] = s_ev.v

                for ttl in range(2):
                    tt = tc * 4 + pr * 2 + ttl

                    @block.sync
                    def _(sync, tt=tt, ttl=ttl, tc=tc, pr=pr):
                        sync.wait_ge(s_st.h, ph5[(tt, "x1st")])
                        if tt >= 1:
                            sync.wait_ge(s_vv.h, ffn.get(("r2", tt - 1), 0))
                        s_ld.inc(sync.dma_start(out=xr2[ttl][:, :],
                                                in_=x1f[tt * 128:(tt + 1) * 128, :]), 16)
                        ffn[("xr2", tt)] = s_ld.v

                    @block.scalar
                    def _(s, tt=tt, ttl=ttl, tc=tc, pr=pr):
                        di_last = (tc * 2 + pr) * 4 + 3
                        s.wait_ge(s_ev.h, ffn[("yev", di_last, ttl)])
                        if tt == 15 and YDUMP:
                            s_st.inc(s.dma_start(out=dbg_h[:, 0:2048], in_=y2[ttl][:, :]), 16)
                            ffn[("ydmp", tt)] = s_st.v
                        s.activation(out=junk3[:, :], in_=y2[ttl][:, :], func=AF.Square,
                                     scale=1.0 / math.sqrt(D), accum_out=stats[:, 3:4])
                        s_ev.inc(s.drain())
                        ffn[("ysq", tt)] = s_ev.v

                    @block.vector
                    def _(v, tt=tt):
                        v.wait_ge(s_ev.h, ffn[("ysq", tt)])
                        v.tensor_copy(stats[:, 14:15], stats[:, 3:4])
                        s_vv.inc(v.engine_nop())
                        ffn[("msq", tt)] = s_vv.v

                    @block.scalar
                    def _(s, tt=tt):
                        s.wait_ge(s_vv.h, ffn[("msq", tt)])
                        s.activation(out=stats[:, 6:7], in_=stats[:, 14:15], func=AF.Ln,
                                     bias=ecol[:, 1:2], scale=1.0)
                        s_ev.inc(s.activation(out=stats[:, 7:8], in_=stats[:, 6:7], func=AF.Exp,
                                              scale=-0.5))
                        ffn[("rrms", tt)] = s_ev.v

                    @block.vector
                    def _(v, tt=tt, ttl=ttl):
                        v.wait_ge(s_ev.h, ffn[("rrms", tt)])
                        v.wait_ge(s_ld.h, ffn[("xr2", tt)])
                        if tt == 15 and YDUMP:
                            v.wait_ge(s_st.h, ffn[("ydmp", tt)])
                        for osp in range(4):
                            v.scalar_tensor_tensor(y2[ttl][:, osp * 512:(osp + 1) * 512],
                                                   y2[ttl][:, osp * 512:(osp + 1) * 512],
                                                   stats[:, 7:8],
                                                   xr2[ttl][:, osp * 512:(osp + 1) * 512],
                                                   OP.mult, OP.add)
                        s_vv.inc(v.engine_nop())
                        ffn[("r2", tt)] = s_vv.v

                    @block.scalar
                    def _(s, tt=tt, ttl=ttl):
                        s.wait_ge(s_vv.h, ffn[("r2", tt)])
                        s.activation(out=junk3[:, :], in_=y2[ttl][:, :], func=AF.Identity,
                                     scale=1.0 / D, accum_out=stats[:, 2:3])
                        s.activation(out=junk3[:, :], in_=y2[ttl][:, :], func=AF.Square,
                                     scale=1.0 / math.sqrt(D), accum_out=stats[:, 3:4])
                        s_ev.inc(s.drain())
                        ffn[("r2sq", tt)] = s_ev.v

                    @block.vector
                    def _(v, tt=tt):
                        v.wait_ge(s_ev.h, ffn[("r2sq", tt)])
                        v.scalar_tensor_tensor(stats[:, 5:6], stats[:, 2:3], stats[:, 2:3],
                                               stats[:, 3:4], OP.mult, OP.subtract)
                        s_vv.inc(v.engine_nop())
                        ffn[("var2", tt)] = s_vv.v

                    @block.scalar
                    def _(s, tt=tt):
                        s.wait_ge(s_vv.h, ffn[("var2", tt)])
                        s.activation(out=stats[:, 6:7], in_=stats[:, 5:6], func=AF.Ln,
                                     bias=ecol[:, 0:1], scale=-1.0)
                        s_ev.inc(s.activation(out=stats[:, 7:8], in_=stats[:, 6:7], func=AF.Exp,
                                              scale=-0.5))
                        ffn[("rstd2", tt)] = s_ev.v
                        if tt == 15 and YDUMP:
                            s.wait_ge(s_ev.h, ffn[("rstd2", tt)])
                            s_st.inc(s.dma_start(out=dbg_h[:, 2048:2064], in_=stats[:, :]), 16)

                    @block.vector
                    def _(v, tt=tt, ttl=ttl):
                        v.wait_ge(s_ev.h, ffn[("rstd2", tt)])
                        v.tensor_scalar(y2[ttl][:, :], y2[ttl][:, :], stats[:, 2:3], stats[:, 7:8],
                                        OP.subtract, OP.mult)
                        s_vv.inc(v.engine_nop())
                        ffn[("o", tt)] = s_vv.v

                    @block.scalar
                    def _(s, tt=tt, ttl=ttl):
                        s.wait_ge(s_vv.h, ffn[("o", tt)])
                        s_out.inc(s.dma_start(out=out_h[tt * 128:(tt + 1) * 128, :],
                                              in_=y2[ttl][:, :]), 16)
                    ffn[("ost", tt)] = s_out.v

        OST_ALL = s_out.v

        @block.sync
        def _(sync):
            sync.wait_ge(s_out.h, OST_ALL)
            if YDUMP:
                s_out.inc(sync.dma_start(out=dbg_h[:, 2064:4112],
                                         in_=hbuf[:, 0:4096].bitcast(F32)), 16)
                s_out.inc(sync.dma_start(out=dbg_h[:, 4112:6160],
                                         in_=x1fm[:, 0:4096].bitcast(F32)), 16)
                sync.wait_ge(s_out.h, s_out.v)
        s3.close()
    return nc


def _host_prep(inputs):
    x = np.ascontiguousarray(np.asarray(inputs["L0_attn_x"], dtype=np.float32)).reshape(16384, D)
    wqkv = np.ascontiguousarray(np.asarray(inputs["L0_attn_w_qkv"], dtype=np.float32))
    wproj = np.ascontiguousarray(np.asarray(inputs["L0_attn_w_proj"], dtype=np.float32))
    bproj = np.ascontiguousarray(np.asarray(inputs["L0_attn_b_proj"], dtype=np.float32)).reshape(1, D)
    gup = np.ascontiguousarray(np.asarray(inputs["L0_ffn_gate_up_packed"], dtype=np.int32))
    gus = np.ascontiguousarray(np.asarray(inputs["L0_ffn_gate_up_scales"], dtype=np.float16))
    dwn = np.ascontiguousarray(np.asarray(inputs["L0_ffn_down_packed"], dtype=np.int32))
    dws = np.ascontiguousarray(np.asarray(inputs["L0_ffn_down_scales"], dtype=np.float16))
    gup16 = gup.view(np.float16)
    dwn16 = dwn.view(np.float16)
    gsrep = np.ascontiguousarray(np.repeat(gus, 32, axis=0))
    dsrep = np.ascontiguousarray(np.repeat(dws, 32, axis=0))
    eye = np.eye(128, dtype=ml_dtypes.bfloat16)
    blk = (np.arange(128)[:, None] // 32 == np.arange(128)[None, :] // 32)
    maskbd = blk.astype(ml_dtypes.bfloat16)
    m4 = (np.arange(128)[:, None] // 32 == np.arange(4)[None, :]).astype(ml_dtypes.bfloat16)
    m4t = np.ascontiguousarray(m4.T)
    onesc = np.ones((128, 1), dtype=ml_dtypes.bfloat16)
    onesr = np.ones((1, 128), dtype=ml_dtypes.bfloat16)
    shared = dict(wqkv=wqkv, wproj=wproj, bproj=bproj, gup16=gup16, dwn16=dwn16,
                  gsrep=gsrep, dsrep=dsrep, eye=eye, maskbd=maskbd, mask4=m4,
                  mask4t=m4t, onesc=onesc, onesr=onesr)
    in_maps = []
    for c in range(8):
        m = dict(shared)
        m["x"] = np.ascontiguousarray(x[c * T:(c + 1) * T])
        in_maps.append(m)
    return in_maps


def kernel(**inputs):
    in_maps = _host_prep(inputs)
    nc = build()
    res = run_bass_kernel_spmd(nc, in_maps, core_ids=list(range(8)))
    out = np.concatenate([np.asarray(res.results[c]["out"]) for c in range(8)], axis=0)
    return out.reshape(4, 4096, D)
